# version 27
# baseline (speedup 1.0000x reference)
import numpy as np
import ml_dtypes
from contextlib import ExitStack
import concourse.bass as bass
import concourse.mybir as mybir
from concourse.bass_utils import run_bass_kernel_spmd

F32 = mybir.dt.float32
BF16 = mybir.dt.bfloat16
AF = mybir.ActivationFunctionType
ALU = mybir.AluOpType

D = 1024
ND = 8
H = 4096
NH = 32
T = 512
NSUP = 8
SEQ_T = 2048
NSEQ = 2
EPS = 1e-5
QEPS = 1e-6

ENGS = ("sync", "tensor", "vector", "scalar", "gpsimd")
DPERM = (0, 1, 6, 2, 3, 7, 4, 5)


class B:

    def __init__(self, nc, es):
        self.nc = nc
        self.es = es
        self.prog = {e: [] for e in ENGS}
        self.semh = {}
        self.semc = {}
        self.maxw = {e: {} for e in ENGS}

    def sem(self, name):
        self.semh[name] = self.es.enter_context(self.nc.semaphore(name))
        self.semc[name] = 0

    def emit(self, eng, fn, inc=None, n=1):
        if inc is not None:
            if inc.startswith("d_") and self.semc[inc] > 0:
                self.wait(eng, (inc, self.semc[inc]))
            self.semc[inc] += n
            tok = self.semc[inc]
            h = self.semh[inc]
            self.prog[eng].append(lambda E, fn=fn, h=h, n=n: fn(E).then_inc(h, n))
            return (inc, tok)
        self.prog[eng].append(fn)
        return None

    def wait(self, eng, tok):
        if tok is None:
            return
        name, v = tok
        assert v <= self.semc[name], f"wait on future token {name}={v}"
        if self.maxw[eng].get(name, 0) >= v:
            return
        self.maxw[eng][name] = v
        h = self.semh[name]
        self.prog[eng].append(lambda E, h=h, v=v: E.wait_ge(h, v))


def build_nc(nsup=NSUP, debug=False):
    nc = bass.Bass("TRN2", target_bir_lowering=False, detect_race_conditions=True)
    es = ExitStack()
    b = B(nc, es)
    for s in ("pe", "v", "s", "g", "d_tiny", "d_x0", "d_x1", "d_sc0", "d_sc1", "d_sc2", "d_sc3",
              "d_cw", "d_wr0", "d_wr1", "d_wi0", "d_wi1", "d_wp0", "d_wp1", "d_wp2",
              "d_stg0", "d_stg1", "d_stg2", "d_stg3", "d_y"):
        b.sem(s)

    Xd = nc.dram_tensor("x", [NSEQ, D, SEQ_T], F32, kind="ExternalInput")
    WRT = nc.dram_tensor("wrt", [D, D], F32, kind="ExternalInput")
    WIT = nc.dram_tensor("wit", [D, D], F32, kind="ExternalInput")
    WFCT = nc.dram_tensor("wfct", [D, H], F32, kind="ExternalInput")
    WPT = nc.dram_tensor("wpt", [H, D], F32, kind="ExternalInput")
    LAM = nc.dram_tensor("lam", [128, ND], F32, kind="ExternalInput")
    LNW = nc.dram_tensor("lnw", [128, NH], F32, kind="ExternalInput")
    LNB = nc.dram_tensor("lnb", [128, NH], F32, kind="ExternalInput")
    Yd = nc.dram_tensor("y", [NSEQ, D, SEQ_T], F32, kind="ExternalOutput")
    wr_sc = nc.dram_tensor("wr_sc", [ND, 128, D], BF16, kind="Internal")
    wi_sc = nc.dram_tensor("wi_sc", [ND, 128, D], BF16, kind="Internal")
    wp_sc = nc.dram_tensor("wp_sc", [ND, 128, H], BF16, kind="Internal")
    row_sc = nc.dram_tensor("row_sc", [2, D], F32, kind="Internal")

    sb = lambda name, shape, dt: es.enter_context(nc.sbuf_tensor(name, shape, dt))
    wfc_q = sb("wfc_q", [128, ND * H], BF16)
    wr_sl = sb("wr_sl", [128, 2 * D], BF16)
    wi_sl = sb("wi_sl", [128, 2 * D], BF16)
    wp_sl = sb("wp_sl", [128, 3 * 2048], BF16)
    xb0 = sb("xb0", [128, ND * T], F32)
    xb1 = sb("xb1", [128, ND * T], F32)
    ln_bf = sb("ln_bf", [128, ND * T], BF16)
    xbr = sb("xbr", [128, 2 * T], BF16)
    sqr = sb("sqr", [128, 2 * T], BF16)
    t_r = sb("t_r", [128, T], F32)
    t_a = sb("t_a", [128, 2 * T], F32)
    t_i = sb("t_i", [128, 2 * T], F32)
    t_q = sb("t_q", [128, 2 * T], F32)
    t_b = sb("t_b", [128, 2 * T], F32)
    rs_t = sb("rs_t", [128, T], F32)
    mb_t = sb("mb_t", [128, T], F32)
    rs3_t = sb("rs3_t", [128, T], F32)
    rm3_t = sb("rm3_t", [128, T], F32)
    g_sb = sb("g_sb", [128, NH * T], BF16)
    gsq = sb("gsq", [128, 3 * T], BF16)
    tmp = sb("tmp", [128, T], F32)
    rows = sb("rows", [1, 3 * T], F32)
    lam_pp = sb("lam_pp", [128, ND], F32)
    clam05 = sb("clam05", [128, ND], F32)
    lnw_pp = sb("lnw_pp", [128, NH], F32)
    lnb_pp = sb("lnb_pp", [128, NH], F32)
    lnb_bf = sb("lnb_bf", [128, NH], BF16)
    c_pp = sb("c_pp", [128, ND], F32)
    wbar_pp = sb("wbar_pp", [128, ND], F32)
    carry = sb("carry", [128, ND], F32)
    ones_bf = sb("ones_bf", [128, 128], BF16)
    ones_f = sb("ones_f", [128, 128], F32)
    ones_kf = sb("ones_kf", [1, 128], F32)
    accf = sb("accf", [128, 32], F32)
    accp = sb("accp", [128, 32], F32)
    thr = sb("thr", [128, 6], F32)
    cst = sb("cst", [128, 8], F32)
    qn_buf = sb("qn_buf", [128, 4 * 1024], BF16)
    lnw_bf = sb("lnw_bf", [128, NH], BF16)
    lnmc_pp = sb("lnmc_pp", [128, ND], F32)
    lnbw_f = sb("lnbw_f", [128, 2 * NH], F32)
    lnbw_bf = sb("lnbw_bf", [128, 2 * NH], BF16)

    ps = lambda name: es.enter_context(nc.psum_tensor(name, [128, T], F32))
    pool = [ps(f"pool{j}") for j in range(4)]
    pY = [ps("pY0"), ps("pY1")]
    pS = ps("pS")
    pQ = ps("pQ")

    stg0 = xb1[:, 0:1024]
    stg1 = xb1[:, 1024:2048]
    xstg = [xb1[:, j * 1024:(j + 1) * 1024] for j in range(4)]
    s1 = rows[0:1, 0:T]
    s2 = rows[0:1, T:2 * T]
    s3 = rows[0:1, 2 * T:3 * T]
    xping = [xb0, xb1]

    free = {}

    def usewait(eng, key):
        if key in free:
            t = free[key]
            for tt in (t if isinstance(t, list) else [t]):
                b.wait(eng, tt)

    def setfree(key, tok):
        free[key] = tok

    xch = lambda buf, c: buf[:, c * T:(c + 1) * T]
    lnch = lambda c: ln_bf[:, c * T:(c + 1) * T]
    gch = lambda h: g_sb[:, h * T:(h + 1) * T]

    tok_x = {}
    b.emit("sync", lambda E: E.dma_start(lam_pp[:, :], LAM[:, :]), inc="d_tiny", n=16)
    b.emit("sync", lambda E: E.dma_start(lnw_pp[:, :], LNW[:, :]), inc="d_tiny", n=16)
    t_tiny = b.emit("sync", lambda E: E.dma_start(lnb_pp[:, :], LNB[:, :]), inc="d_tiny", n=16)
    for k in (0,):
        seq, t0 = k // 4, (k % 4) * T
        dst = xping[k % 2][:, :].rearrange("p (c t) -> p c t", c=ND)
        src = Xd[seq, :, t0:t0 + T].rearrange("(c p) t -> p c t", p=128)
        tok_x[k] = b.emit("gpsimd", lambda E, d=dst, s=src: E.dma_start(d, s), inc=f"d_x{k % 2}", n=16)

    b.emit("vector", lambda E: E.memset(cst[:, 0:1], 0.5))
    b.emit("vector", lambda E: E.memset(cst[:, 1:2], -0.5))
    b.emit("vector", lambda E: E.memset(cst[:, 2:3], 1.0 / 1024.0))
    b.emit("vector", lambda E: E.memset(cst[:, 3:4], 1.0 / 4096.0))
    b.emit("vector", lambda E: E.memset(cst[:, 4:5], EPS))
    b.emit("vector", lambda E: E.memset(cst[:, 5:6], 4.0))
    b.emit("vector", lambda E: E.memset(cst[:, 6:7], -8.0))
    b.emit("vector", lambda E: E.memset(ones_bf[:, :], 1.0))
    b.emit("vector", lambda E: E.memset(ones_f[:, :], 1.0))
    tok_ones = b.emit("vector", lambda E: E.memset(ones_kf[:, :], 1.0), inc="v")
    b.wait("vector", t_tiny)
    b.emit("vector", lambda E: E.tensor_copy(lnb_bf[:, :], lnb_pp[:, :]))
    tok_lnbbf = b.emit("vector", lambda E: E.tensor_copy(lnw_bf[:, :], lnw_pp[:, :]), inc="v")
    lnbw_v = lnbw_f.rearrange("p (c two) -> p c two", two=2)
    b.emit("vector", lambda E: E.tensor_copy(lnbw_v[:, :, 0], lnb_pp[:, :]))
    tok_lnbw0 = b.emit("vector", lambda E: E.tensor_copy(lnbw_v[:, :, 1], lnw_pp[:, :]), inc="v")
    b.wait("vector", tok_lnbw0)
    tok_lnbw = b.emit("vector", lambda E: E.tensor_copy(lnbw_bf[:, :], lnbw_f[:, :]), inc="v")
    b.wait("scalar", t_tiny)
    b.wait("scalar", tok_ones)
    tA = b.emit("scalar", lambda E: E.activation(clam05[:, :], lam_pp[:, :], AF.Exp, scale=-1.0), inc="s")
    b.wait("scalar", tA)
    tA = b.emit("scalar", lambda E: E.activation(clam05[:, :], clam05[:, :], AF.Ln, bias=1.0), inc="s")
    b.wait("scalar", tA)
    tA = b.emit("scalar", lambda E: E.activation(clam05[:, :], clam05[:, :], AF.Copy, scale=cst[:, 6:7]), inc="s")
    b.wait("scalar", tA)
    tok_clam = b.emit("scalar", lambda E: E.activation(lnmc_pp[:, :], clam05[:, :], AF.Ln, scale=-1.0), inc="s")

    qn_s = lambda i: qn_buf[:, (i % 4) * 1024:((i % 4) + 1) * 1024]
    sc_last = {}

    def emit_scatter(slot, dst, src_ap):
        sem = f"d_sc{slot % 4}"
        qeng = "sync" if slot % 2 == 0 else "gpsimd"
        tw = b.emit(qeng, lambda E, d=dst, s=src_ap: E.dma_start(d, s), inc=sem, n=16)
        setfree(("qn4", slot % 4), tw)
        sc_last[sem] = tw
        return tw

    for widx, (WSRC, WSC) in enumerate(((WRT, wr_sc), (WIT, wi_sc))):
        for kd in range(ND):
            slot = (widx * ND + kd) % 4
            st = stg0 if kd % 2 == 0 else stg1
            qeng = "sync" if kd % 2 == 0 else "gpsimd"
            usewait(qeng, ("stg", kd % 2))
            tl = b.emit(qeng, lambda E, st=st, W=WSRC, kd=kd:
                        E.dma_start(st, W[128 * kd:128 * (kd + 1), :]), inc=f"d_stg{kd % 2}", n=16)
            b.wait("vector", tl)
            usewait("vector", ("qn4", slot))
            tc = b.emit("vector", lambda E, sl=qn_s(slot), st=st: E.tensor_copy(sl, st), inc="v")
            setfree(("stg", kd % 2), tc)
            qw = "sync" if slot % 2 == 0 else "gpsimd"
            b.wait(qw, tc)
            dst = WSC[:, :, kd * 128:(kd + 1) * 128].transpose([1, 0, 2])
            src2 = qn_s(slot).rearrange("p (ke j) -> p ke j", ke=ND)
            emit_scatter(slot, dst, src2)
    tok_wrsc = list(sc_last.values())
    tok_wisc = []

    tstg = [t_a[:, :], t_q[:, :], t_b[:, :], t_i[:, :]]
    tkeymap = ["t_a", "t_q", "t_b", "t_i"]
    gfc_state = {"last": None}
    tok_wfc_pc = {}
    p2b_defer = {}

    gfc_toks = {}

    def emit_gfc_loads(qi):
        for j in range(4):
            i = 4 * qi + j
            pc, kd = i // 8, i % 8
            sl = i % 4
            st = xstg[sl]
            qeng = "sync" if sl % 2 == 0 else "gpsimd"
            usewait(qeng, ("stg", sl))
            gfc_toks[i] = b.emit(qeng, lambda E, st=st, kd=kd, pc=pc:
                                 E.dma_start(st, WFCT[128 * kd:128 * (kd + 1), 1024 * pc:1024 * (pc + 1)]),
                                 inc=f"d_stg{sl}", n=16)

    def emit_gfc_abs(qi):
        for j in range(4):
            i = 4 * qi + j
            sl = i % 4
            st = xstg[sl]
            b.wait("scalar", gfc_toks[i])
            red = b.emit("scalar", lambda E, st=st, i=i:
                         E.activation(st, st, AF.Abs, accum_out=accf[:, i:i + 1]), inc="s")
            setfree(("stg", sl), red)
            gfc_state["last"] = red

    def emit_thr(acc, pool_bank, col):
        usewait("tensor", ("pool", pool_bank))
        tg = b.emit("tensor", lambda E: E.matmul(pool[pool_bank][:, 0:32], ones_f[:, :], acc[:, 0:32],
                                                 start=True, stop=True), inc="pe")
        b.wait("vector", tg)
        tA = b.emit("vector", lambda E: E.tensor_reduce(thr[:, col:col + 1], pool[pool_bank][:, 0:32],
                                                        mybir.AxisListType.X, ALU.add), inc="v")
        setfree(("pool", pool_bank), tA)
        b.wait("vector", tA)
        tA = b.emit("vector", lambda E: E.tensor_scalar(thr[:, col:col + 1], thr[:, col:col + 1],
                                                        1.0 / (H * D), QEPS, ALU.mult, ALU.add), inc="v")
        b.wait("vector", tA)
        b.emit("vector", lambda E: E.tensor_scalar_mul(thr[:, col + 1:col + 2], thr[:, col:col + 1], 0.5))
        return b.emit("vector", lambda E: E.tensor_scalar_mul(thr[:, col + 2:col + 3], thr[:, col:col + 1], -0.5), inc="v")

    def emit_ln2_tile_one(c, xb, tok_bc2, tok_i7, tok_gbl):
        eng, scr, inc, key = (("vector", tmp, "v", ("tmp",)) if c < 6 else
                              ("gpsimd", t_r, "g", ("t_r",)))
        b.wait(eng, tok_bc2)
        b.wait(eng, tok_i7)
        b.wait(eng, tok_gbl)
        usewait(eng, ("lnbf",))
        usewait(eng, key)
        tA = b.emit(eng, lambda E, c=c, xb=xb, scr=scr: E.tensor_tensor(scr[:, :], xch(xb, c), rs_t[:, :], ALU.mult), inc=inc)
        b.wait(eng, tA)
        tln2 = b.emit(eng, lambda E, c=c, scr=scr: E.tensor_tensor(lnch(c), scr[:, :], mb_t[:, :], ALU.add), inc=inc)
        setfree(key, tln2)
        return tln2

    def emit_prep2a_mid(xb, tok_bc2, tok_i7, tok_gbl):
        nonlocal tok_wfc, tok_thrp
        ln2_toks = {}
        b.wait("tensor", gfc_state["last"])
        b.wait("tensor", tok_ones)
        tok_thrf = emit_thr(accf, 0, 0)
        red_p = None
        for i in range(32):
            pc, kd = i // 8, i % 8
            sl = i % 4
            st = xstg[sl]
            qeng = "sync" if sl % 2 == 0 else "gpsimd"
            usewait(qeng, ("stg", sl))
            tl = b.emit(qeng, lambda E, st=st, kd=kd, pc=pc:
                        E.dma_start(st, WFCT[128 * kd:128 * (kd + 1), 1024 * pc:1024 * (pc + 1)]),
                        inc=f"d_stg{sl}", n=16)
            st2 = tstg[sl]
            qeng2 = "sync" if sl % 2 == 0 else "gpsimd"
            if i < 4:
                usewait(qeng2, (tkeymap[sl], 0))
                usewait(qeng2, (tkeymap[sl], 1))
            else:
                usewait(qeng2, ("tstg", sl))
            tl2 = b.emit(qeng2, lambda E, st2=st2, i=i:
                         E.dma_start(st2, WPT[128 * i:128 * (i + 1), :]), inc=f"d_stg{sl}", n=16)
            b.wait("vector", tl)
            b.wait("vector", tok_thrf)
            usewait("vector", ("qn4", sl))
            tA2 = b.emit("vector", lambda E, st=st, sl=sl:
                         E.tensor_single_scalar(qn_s(sl), st, thr[:, 2:3], ALU.is_le), inc="v")
            b.wait("vector", tA2)
            dstq = wfc_q[:, kd * H + pc * 1024: kd * H + (pc + 1) * 1024]
            tq = b.emit("vector", lambda E, st=st, d=dstq, sl=sl:
                        E.scalar_tensor_tensor(d, st, thr[:, 1:2], qn_s(sl),
                                               ALU.is_ge, ALU.subtract), inc="v")
            setfree(("qn4", sl), tq)
            setfree(("stg", sl), tq)
            if kd == 7:
                tok_wfc_pc[pc] = tq
            b.wait("scalar", tl2)
            red_p = b.emit("scalar", lambda E, st2=st2, i=i:
                           E.activation(st2, st2, AF.Abs, accum_out=accp[:, i:i + 1]), inc="s")
            setfree(("tstg", sl), red_p)
            if i % 2 == 0 and i // 2 < ND:
                ln2_toks[i // 2] = emit_ln2_tile_one(i // 2, xb, tok_bc2, tok_i7, tok_gbl)
        tok_wfc = tok_wfc_pc[3]
        b.wait("tensor", red_p)
        tok_thrp = emit_thr(accp, 1, 3)
        for sl in range(4):
            usewait("gpsimd", ("stg", sl))
        dst1 = xping[1][:, :].rearrange("p (c t) -> p c t", c=ND)
        src1 = Xd[0, :, T:2 * T].rearrange("(c p) t -> p c t", p=128)
        tok_x[1] = b.emit("gpsimd", lambda E, d=dst1, s=src1: E.dma_start(d, s), inc="d_x1", n=16)
        return [ln2_toks[c] for c in range(ND)]

    p2b_loads = {}

    def emit_prep2b_load(i):
        sl = i % 4
        st = tstg[sl]
        qeng = "sync" if sl % 2 == 0 else "gpsimd"
        usewait(qeng, ("tstg", sl))
        p2b_loads[i] = b.emit(qeng, lambda E, st=st, i=i:
                              E.dma_start(st, WPT[128 * i:128 * (i + 1), :]), inc=f"d_stg{sl}", n=16)

    def emit_prep2b_piece(i):
        nonlocal tok_cw
        sl = i % 4
        st = tstg[sl]
        tl = p2b_loads.pop(i)
        b.wait("vector", tl)
        b.wait("vector", tok_thrp)
        usewait("vector", ("qn4", sl))
        tA4 = b.emit("vector", lambda E, st=st, sl=sl:
                     E.tensor_single_scalar(qn_s(sl), st, thr[:, 5:6], ALU.is_le), inc="v")
        b.wait("vector", tA4)
        tq1 = b.emit("vector", lambda E, st=st, sl=sl:
                     E.scalar_tensor_tensor(qn_s(sl), st, thr[:, 4:5], qn_s(sl),
                                            ALU.is_ge, ALU.subtract), inc="v")
        setfree(("tstg", sl), tq1)
        b.wait("tensor", tq1)
        b.wait("tensor", tok_lnbw)
        qh = qn_s(sl)
        b.emit("tensor", lambda E, i=i, qh=qh: E.matmul(pY[0][0:2, :], lnbw_bf[:, 2 * i:2 * i + 2],
                                                        qh[:, 0:512], start=(i == 0), stop=(i == 31)))
        tw2 = b.emit("tensor", lambda E, i=i, qh=qh: E.matmul(pY[1][0:2, :], lnbw_bf[:, 2 * i:2 * i + 2],
                                                              qh[:, 512:1024], start=(i == 0), stop=(i == 31)),
                     inc="pe")
        if i == 31:
            tok_cw = tw2
        p2b_defer[i] = (qh, tw2)

    def emit_prep2b_fold(i):
        qh, tw2 = p2b_defer.pop(i)
        b.wait("vector", tw2)
        tq2 = b.emit("vector", lambda E, i=i, qh=qh:
                     E.tensor_scalar_mul(qh, qh, lnw_pp[:, i:i + 1]), inc="v")
        qw = "sync" if i % 2 == 0 else "gpsimd"
        b.wait(qw, tq2)
        dst = wp_sc[:, :, i * 128:(i + 1) * 128].transpose([1, 0, 2])
        src2 = qh.rearrange("p (kd j) -> p kd j", kd=ND)
        emit_scatter(i, dst, src2)

    def emit_prep2b_tail():
        nonlocal tok_cwe, tok_cpp, tok_wpsc
        emit_prep2b_fold(NH - 2)
        emit_prep2b_fold(NH - 1)
        tok_wpsc = [sc_last[s] for s in ("d_sc0", "d_sc1", "d_sc2", "d_sc3")]
        for sl in range(4):
            tok = free[("tstg", sl)]
            setfree((tkeymap[sl], 0), tok)
            setfree((tkeymap[sl], 1), tok)
        b.wait("vector", tok_cw)
        usewait("vector", ("tmp",))
        b.emit("vector", lambda E: E.tensor_copy(tmp[0:2, :], pY[0][0:2, :]))
        tok_cwe = b.emit("vector", lambda E: E.tensor_copy(tmp[32:34, :], pY[1][0:2, :]), inc="v")
        setfree(("pY", 0), tok_cwe)
        setfree(("pY", 1), tok_cwe)
        b.wait("sync", tok_cwe)
        b.emit("sync", lambda E: E.dma_start(row_sc[0:1, 0:512], tmp[0:1, :]), inc="d_cw", n=16)
        b.emit("sync", lambda E: E.dma_start(row_sc[0:1, 512:1024], tmp[32:33, :]), inc="d_cw", n=16)
        b.emit("sync", lambda E: E.dma_start(row_sc[1:2, 0:512], tmp[1:2, :]), inc="d_cw", n=16)
        trw = b.emit("sync", lambda E: E.dma_start(row_sc[1:2, 512:1024], tmp[33:34, :]), inc="d_cw", n=16)
        b.wait("sync", trw)
        b.emit("sync", lambda E: E.dma_start(c_pp[:, :], row_sc[0, :].rearrange("(c p) -> p c", p=128),
                                             allow_slow_non_contiguous=True), inc="d_cw", n=16)
        tok_cpp = b.emit("sync", lambda E: E.dma_start(wbar_pp[:, :], row_sc[1, :].rearrange("(c p) -> p c", p=128),
                                                       allow_slow_non_contiguous=True), inc="d_cw", n=16)
        setfree(("tmp",), tok_cpp)

    tok_wfc = tok_thrp = tok_cw = tok_cwe = tok_cpp = None
    tok_wpsc = []

    def ln_stats_mm(xbuf):
        for c in range(ND):
            sl = c % 2
            usewait("vector", ("xbr", sl))
            tA = b.emit("vector", lambda E, c=c, sl=sl, xb=xbuf: E.tensor_copy(xbr[:, sl * T:(sl + 1) * T], xch(xb, c)), inc="v")
            b.wait("vector", tA)
            tsq = b.emit("vector", lambda E, sl=sl:
                         E.tensor_tensor(sqr[:, sl * T:(sl + 1) * T], xbr[:, sl * T:(sl + 1) * T],
                                         xbr[:, sl * T:(sl + 1) * T], ALU.mult), inc="v")
            b.wait("tensor", tsq)
            if c == 0:
                usewait("tensor", ("pS",))
                usewait("tensor", ("pQ",))
            b.emit("tensor", lambda E, c=c, sl=sl: E.matmul(pS[:, :], ones_bf[:, :], xbr[:, sl * T:(sl + 1) * T],
                                                            start=(c == 0), stop=(c == ND - 1)))
            tst = b.emit("tensor", lambda E, c=c, sl=sl: E.matmul(pQ[:, :], ones_bf[:, :], sqr[:, sl * T:(sl + 1) * T],
                                                                  start=(c == 0), stop=(c == ND - 1)), inc="pe")
            setfree(("xbr", sl), tst)
        return tst

    def rows_and_bcast(tstats, divcol, RS, MB, prev_read=None, prev_read_g=None):
        b.wait("scalar", tstats)
        usewait("scalar", ("rows",))
        b.emit("scalar", lambda E: E.activation(s1, pS[0:1, :], AF.Copy, scale=cst[0:1, divcol:divcol + 1]))
        trow = b.emit("scalar", lambda E: E.activation(s2, pQ[0:1, :], AF.Copy, scale=cst[0:1, divcol:divcol + 1]), inc="s")
        b.wait("vector", trow)
        tA = b.emit("vector", lambda E: E.tensor_tensor(s3, s1, s1, ALU.mult), inc="v")
        b.wait("vector", tA)
        tv1 = b.emit("vector", lambda E: E.scalar_tensor_tensor(s3, s3, -1.0, s2, ALU.mult, ALU.add), inc="v")
        b.wait("scalar", tv1)
        tA = b.emit("scalar", lambda E: E.activation(s3, s3, AF.Ln, bias=cst[0:1, 4:5]), inc="s")
        b.wait("scalar", tA)
        ts2 = b.emit("scalar", lambda E: E.activation(s2, s3, AF.Exp, scale=cst[0:1, 1:2]), inc="s")
        b.wait("vector", ts2)
        trow2 = b.emit("vector", lambda E: E.scalar_tensor_tensor(s1, s1, -1.0, s2, ALU.mult, ALU.mult), inc="v")
        b.wait("tensor", trow2)
        tb1 = b.emit("tensor", lambda E: E.matmul(pS[:, :], ones_kf[:, :], s2, start=True, stop=True), inc="pe")
        tb2 = b.emit("tensor", lambda E: E.matmul(pQ[:, :], ones_kf[:, :], s1, start=True, stop=True), inc="pe")
        setfree(("rows",), tb2)
        b.wait("vector", tb1)
        if prev_read is not None:
            b.wait("vector", prev_read)
        if prev_read_g is not None:
            b.wait("vector", prev_read_g)
        b.emit("vector", lambda E, R=RS: E.tensor_copy(R[:, :], pS[:, :]), inc="v")
        b.wait("vector", tb2)
        tbc = b.emit("vector", lambda E, M=MB: E.tensor_copy(M[:, :], pQ[:, :]), inc="v")
        setfree(("pS",), tbc)
        setfree(("pQ",), tbc)
        return tbc

    def emit_y_group(j, d):
        bank = d % 2
        usewait("tensor", ("pY", bank))
        tok = None
        for half in range(2):
            hh = d * 2 + half
            b.wait("tensor", tok_wp[(j, hh)])
            for kk in range(16):
                kh = half * 16 + kk
                sl = hh % 3
                last = (half == 1 and kk == 15)
                fn = lambda E, kh=kh, sl=sl, bank=bank, first=(kh == 0), last=last: E.matmul(
                    pY[bank][:, :], wp_sl[:, sl * 2048 + (kh % 16) * 128: sl * 2048 + (kh % 16) * 128 + 128],
                    gch(kh), start=first, stop=last)
                if kk == 15:
                    tok = b.emit("tensor", fn, inc="pe")
                    setfree(("wpsl", sl), tok)
                else:
                    b.emit("tensor", fn)
        tok_ymm[(j, d)] = tok
        for hh2 in (2 * d + 2, 2 * d + 3):
            if hh2 < 2 * ND and (j, hh2) not in tok_wp:
                emit_wp_load(j, hh2)

    def emit_y_evac(j, d):
        bank = d % 2
        xb = xping[j % 2]
        b.wait("vector", tok_ymm[(j, d)])
        b.wait("vector", tok_bc3_of[j])
        b.wait("vector", tok_cpp)
        usewait("vector", ("tmp",))
        tA = b.emit("vector", lambda E, bank=bank: E.tensor_tensor(tmp[:, :], pY[bank][:, :], rs3_t[:, :], ALU.mult), inc="v")
        b.wait("vector", tA)
        tfree = b.emit("vector", lambda E, d=d: E.scalar_tensor_tensor(
            tmp[:, :], rm3_t[:, :], wbar_pp[:, d:d + 1], tmp[:, :], ALU.mult, ALU.add), inc="v")
        setfree(("pY", bank), tfree)
        b.wait("vector", tfree)
        tok = b.emit("vector", lambda E, d=d, xb=xb: E.scalar_tensor_tensor(
            xch(xb, d), tmp[:, :], c_pp[:, d:d + 1], xch(xb, d), ALU.add, ALU.add), inc="v")
        setfree(("tmp",), tok)
        tok_yev[(j, d)] = tok

    def emit_wri_load(k, e):
        sl = (k * ND + e) % 2
        usewait("sync", ("wrsl", sl))
        for t_ in tok_wrsc:
            b.wait("sync", t_)
        tok_wr[(k, e, 0)] = b.emit("sync", lambda E, e=e, sl=sl:
                                   E.dma_start(wr_sl[:, sl * D:(sl + 1) * D], wr_sc[e, :, :]), inc=f"d_wr{sl}", n=16)
        usewait("sync", ("wisl", sl))
        for t_ in tok_wisc:
            b.wait("sync", t_)
        tok_wr[(k, e, 1)] = b.emit("sync", lambda E, e=e, sl=sl:
                                   E.dma_start(wi_sl[:, sl * D:(sl + 1) * D], wi_sc[e, :, :]), inc=f"d_wi{sl}", n=16)

    def emit_wp_load(j, hh):
        d, half = hh // 2, hh % 2
        sl = hh % 3
        usewait("sync", ("wpsl", sl))
        for t_ in tok_wpsc:
            b.wait("sync", t_)
        tok_wp[(j, hh)] = b.emit("sync", lambda E, d=d, half=half, sl=sl:
                                 E.dma_start(wp_sl[:, sl * 2048:(sl + 1) * 2048],
                                             wp_sc[d, :, half * 2048:(half + 1) * 2048]), inc=f"d_wp{sl}", n=16)

    tok_wr = {}
    tok_wp = {}
    tok_ri = {}
    tok_ymm = {}
    tok_yev = {}
    tok_ystore = {}
    tok_gelu = {}

    tok_bc3_of = {}
    tok_ln1_of = {}
    tok_ln2_of = {}

    tok_x1_of = {}

    def emit_ln2_stat_chunk(k, xbuf, c):
        sl = c % 2
        eng, inc = ("gpsimd", "g")
        b.wait(eng, tok_x1_of[c])
        usewait(eng, ("xbr", sl))
        tA = b.emit(eng, lambda E, c=c, sl=sl, xb=xbuf: E.tensor_copy(xbr[:, sl * T:(sl + 1) * T], xch(xb, c)), inc=inc)
        b.wait(eng, tA)
        tsq = b.emit(eng, lambda E, sl=sl:
                     E.tensor_tensor(sqr[:, sl * T:(sl + 1) * T], xbr[:, sl * T:(sl + 1) * T],
                                     xbr[:, sl * T:(sl + 1) * T], ALU.mult), inc=inc)
        b.wait("tensor", tsq)
        if c == 0:
            usewait("tensor", ("pS",))
            usewait("tensor", ("pQ",))
        b.emit("tensor", lambda E, c=c, sl=sl: E.matmul(pS[:, :], ones_bf[:, :], xbr[:, sl * T:(sl + 1) * T],
                                                        start=(c == 0), stop=(c == ND - 1)))
        tst = b.emit("tensor", lambda E, c=c, sl=sl: E.matmul(pQ[:, :], ones_bf[:, :], sqr[:, sl * T:(sl + 1) * T],
                                                              start=(c == 0), stop=(c == ND - 1)), inc="pe")
        setfree(("xbr", sl), tst)
        return tst

    def emit_ln1(k):
        xb = xping[k % 2]
        b.wait("vector", tok_x[k])
        tstat1 = ln_stats_mm(xb)
        pr = tok_ln2_of.get(k - 1)
        tok_bc1 = rows_and_bcast(tstat1, 2, rs_t, mb_t, prev_read=pr[5] if pr else None,
                                 prev_read_g=pr[7] if pr else None)
        lt = []
        for c in range(ND):
            eng, scr, inc, key = (("vector", tmp, "v", ("tmp",)) if c < 6 else
                                  ("gpsimd", t_r, "g", ("t_r",)))
            b.wait(eng, tok_bc1)
            usewait(eng, ("lnbf",))
            usewait(eng, key)
            tA = b.emit(eng, lambda E, c=c, xb=xb, scr=scr: E.tensor_tensor(scr[:, :], xch(xb, c), rs_t[:, :], ALU.mult), inc=inc)
            b.wait(eng, tA)
            tln = b.emit(eng, lambda E, c=c, scr=scr: E.tensor_tensor(lnch(c), scr[:, :], mb_t[:, :], ALU.add), inc=inc)
            setfree(key, tln)
            lt.append(tln)
        tok_ln1_of[k] = lt
        free.pop(("lnbf",), None)

    emit_ln1(0)

    for k in range(nsup):
        xb = xping[k % 2]
        seq, t0 = k // 4, (k % 4) * T
        ln1lt = tok_ln1_of[k]
        tok_ln1, tok_ln1g = ln1lt[5], ln1lt[7]

        for e in (0, 1):
            emit_wri_load(k, e)
        if k == 0:
            emit_gfc_loads(0)

        for e in range(ND):
            q = e % 2
            pR = pool[2 * q]
            pI = pool[2 * q + 1]
            b.wait("tensor", tok_wr[(k, e, 0)])
            usewait("tensor", ("pool", 2 * q))
            sl = (k * ND + e) % 2
            for ii, d in enumerate(DPERM):
                b.wait("tensor", ln1lt[d])
                fn = lambda E, d=d, sl=sl, pR=pR, ii=ii: E.matmul(
                    pR[:, :], wr_sl[:, sl * D + d * 128: sl * D + d * 128 + 128], lnch(d),
                    start=(ii == 0), stop=(ii == ND - 1))
                if ii == ND - 1:
                    tok_r = b.emit("tensor", fn, inc="pe")
                    setfree(("wrsl", sl), tok_r)
                else:
                    b.emit("tensor", fn)
            b.wait("tensor", tok_wr[(k, e, 1)])
            usewait("tensor", ("pool", 2 * q + 1))
            for ii, d in enumerate(DPERM):
                fn = lambda E, d=d, sl=sl, pI=pI, ii=ii: E.matmul(
                    pI[:, :], wi_sl[:, sl * D + d * 128: sl * D + d * 128 + 128], lnch(d),
                    start=(ii == 0), stop=(ii == ND - 1))
                if ii == ND - 1:
                    tok_i = b.emit("tensor", fn, inc="pe")
                    setfree(("wisl", sl), tok_i)
                else:
                    b.emit("tensor", fn)
            tok_ri[(k, e)] = (tok_r, tok_i)
            if e + 2 < ND:
                emit_wri_load(k, e + 2)
            if k >= 1 and 1 <= e <= 7:
                emit_y_evac(k - 1, e - 1)
            if k >= 1 and 1 <= e <= 6:
                emit_y_group(k - 1, e + 1)
            if k == 0:
                emit_gfc_abs(e)
                if e + 1 < ND:
                    emit_gfc_loads(e + 1)

            b.wait("scalar", tok_r)
            b.wait("scalar", tok_clam)
            usewait("scalar", ("t_a", q))
            usewait("scalar", ("t_r",))
            tA = b.emit("scalar", lambda E, pR=pR: E.activation(t_r[:, :], pR[:, :], AF.Exp, scale=-1.0), inc="s")
            setfree(("pool", 2 * q), tA)
            b.wait("scalar", tA)
            tA = b.emit("scalar", lambda E: E.activation(t_r[:, :], t_r[:, :], AF.Ln, bias=1.0), inc="s")
            b.wait("scalar", tA)
            usewait("scalar", ("t_a", q))
            tA = b.emit("scalar", lambda E, q=q, e=e: E.activation(
                t_a[:, q * T:(q + 1) * T], t_r[:, :], AF.Exp, scale=-1.0,
                bias=lnmc_pp[:, e:e + 1]), inc="s")
            setfree(("t_r",), tA)
            b.wait("scalar", tA)
            tfR = b.emit("scalar", lambda E, q=q: E.activation(
                t_a[:, q * T:(q + 1) * T], t_a[:, q * T:(q + 1) * T], AF.Exp, scale=-1.0), inc="s")
            b.wait("scalar", tok_i)
            usewait("scalar", ("t_i", q))
            tA = b.emit("scalar", lambda E, pI=pI, q=q: E.activation(
                t_i[:, q * T:(q + 1) * T], pI[:, :], AF.Exp, scale=-1.0), inc="s")
            setfree(("pool", 2 * q + 1), tA)
            b.wait("scalar", tA)
            tvi = b.emit("scalar", lambda E, q=q: E.activation(
                t_i[:, q * T:(q + 1) * T], t_i[:, q * T:(q + 1) * T], AF.Ln, bias=1.0), inc="s")
            b.wait("vector", tfR)
            usewait("vector", ("t_q", q))
            tv1 = b.emit("vector", lambda E, q=q: E.scalar_tensor_tensor(
                t_q[:, q * T:(q + 1) * T], t_a[:, q * T:(q + 1) * T], -1.0,
                t_a[:, q * T:(q + 1) * T], ALU.mult, ALU.mult), inc="v")
            b.wait("scalar", tv1)
            tA = b.emit("scalar", lambda E, q=q: E.activation(
                t_q[:, q * T:(q + 1) * T], t_q[:, q * T:(q + 1) * T], AF.Ln, bias=1.0), inc="s")
            b.wait("vector", tA)
            b.wait("vector", tvi)
            tvm = b.emit("vector", lambda E, q=q: E.scalar_tensor_tensor(
                t_q[:, q * T:(q + 1) * T], t_q[:, q * T:(q + 1) * T], 0.5,
                t_i[:, q * T:(q + 1) * T], ALU.mult, ALU.subtract), inc="v")
            setfree(("t_i", q), tvm)
            b.wait("scalar", tvm)
            tst = b.emit("scalar", lambda E, q=q: E.activation(
                t_q[:, q * T:(q + 1) * T], t_q[:, q * T:(q + 1) * T], AF.Exp), inc="s")
            b.wait("vector", tst)
            b.wait("vector", ln1lt[e])
            usewait("vector", ("t_b", q))
            tgb = b.emit("vector", lambda E, q=q, e=e: E.tensor_tensor(
                t_b[:, q * T:(q + 1) * T], t_q[:, q * T:(q + 1) * T], lnch(e), ALU.mult), inc="v")
            setfree(("t_q", q), tgb)
            b.wait("vector", tgb)
            init = 0.0 if (k % 4 == 0) else carry[:, e:e + 1]
            tA = b.emit("vector", lambda E, q=q, init=init: E.tensor_tensor_scan(
                t_b[:, q * T:(q + 1) * T], t_a[:, q * T:(q + 1) * T], t_b[:, q * T:(q + 1) * T],
                init, ALU.mult, ALU.add), inc="v")
            b.wait("vector", tA)
            b.emit("vector", lambda E, q=q, e=e: E.tensor_copy(carry[:, e:e + 1], t_b[:, q * T + T - 1: q * T + T]))
            tx1 = b.emit("vector", lambda E, q=q, e=e, xb=xb: E.tensor_tensor(
                xch(xb, e), xch(xb, e), t_b[:, q * T:(q + 1) * T], ALU.add), inc="v")
            setfree(("t_a", q), tx1)
            setfree(("t_b", q), tx1)
            tok_gb_last = tgb
            tok_x1_of[e] = tx1
            if e >= 1:
                emit_ln2_stat_chunk(k, xb, e - 1)

        tstat2 = emit_ln2_stat_chunk(k, xb, ND - 1)
        tok_bc2 = rows_and_bcast(tstat2, 2, rs_t, mb_t, prev_read=tok_ln1, prev_read_g=tok_ln1g)
        if k == 0:
            tok_ln2 = emit_prep2a_mid(xb, tok_bc2, tok_ri[(k, ND - 1)][1], tok_x1_of[ND - 1])
        else:
            lt = [emit_ln2_tile_one(c, xb, tok_bc2, tok_ri[(k, ND - 1)][1], tok_x1_of[ND - 1])
                  for c in range(ND)]
            tok_ln2 = lt
        tok_ln2_of[k] = tok_ln2
        if k >= 1:
            emit_y_evac(k - 1, 7)
            j = k - 1
            b.wait("gpsimd", tok_yev[(j, ND - 1)])
            sj, tj = j // 4, (j % 4) * T
            src = xping[j % 2][:, :].rearrange("p (c t) -> p c t", c=ND)
            dst = Yd[sj, :, tj:tj + T].rearrange("(c p) t -> p c t", p=128)
            tok_ystore[j] = b.emit("gpsimd", lambda E, d=dst, s=src: E.dma_start(d, s), inc="d_y", n=16)
            if k + 1 < nsup:
                kk = k + 1
                b.wait("gpsimd", tok_ystore.get(kk - 2, None))
                sq2, t02 = kk // 4, (kk % 4) * T
                dst2 = xping[kk % 2][:, :].rearrange("p (c t) -> p c t", c=ND)
                src2 = Xd[sq2, :, t02:t02 + T].rearrange("(c p) t -> p c t", p=128)
                tok_x[kk] = b.emit("gpsimd", lambda E, d=dst2, s=src2: E.dma_start(d, s), inc=f"d_x{kk % 2}", n=16)

        if k > 0:
            b.wait("tensor", tok_wfc)
        else:
            for i3 in range(3):
                emit_prep2b_load(i3)
        gs_toks = {}

        def emit_ln3_stat(h):
            tge2, tgs2 = gs_toks[h]
            b.wait("tensor", tge2)
            b.wait("tensor", tgs2)
            if h == 0:
                usewait("tensor", ("pS",))
                usewait("tensor", ("pQ",))
            b.emit("tensor", lambda E, h=h: E.matmul(pS[:, :], ones_bf[:, :], gch(h),
                                                     start=(h == 0), stop=(h == NH - 1)))
            tq = b.emit("tensor", lambda E, h=h: E.matmul(pQ[:, :], ones_bf[:, :],
                                                          gsq[:, (h % 3) * T:(h % 3 + 1) * T],
                                                          start=(h == 0), stop=(h == NH - 1)), inc="pe")
            setfree(("gsq", h % 3), tq)
            return tq

        for h in range(NH):
            bank = h % 4
            if k == 0:
                b.wait("tensor", tok_wfc_pc[h // 8])
            usewait("tensor", ("pool", bank))
            for ii, d in enumerate(DPERM):
                b.wait("tensor", tok_ln2[d])
                fn = lambda E, h=h, d=d, bank=bank, ii=ii: E.matmul(
                    pool[bank][:, :], wfc_q[:, d * H + h * 128: d * H + h * 128 + 128], lnch(d),
                    start=(ii == 0), stop=(ii == ND - 1))
                if ii == ND - 1:
                    tu = b.emit("tensor", fn, inc="pe")
                else:
                    b.emit("tensor", fn)
            b.wait("scalar", tu)
            tge = b.emit("scalar", lambda E, h=h, bank=bank: E.activation(
                gch(h), pool[bank][:, :], AF.Gelu), inc="s")
            setfree(("pool", bank), tge)
            tok_gelu[(k, h)] = tge
            geng, ginc = ("gpsimd", "g") if k == 0 else ("vector", "v")
            b.wait(geng, tge)
            usewait(geng, ("gsq", h % 3))
            tgs = b.emit(geng, lambda E, h=h: E.tensor_tensor(
                gsq[:, (h % 3) * T:(h % 3 + 1) * T], gch(h), gch(h), ALU.mult), inc=ginc)
            gs_toks[h] = (tge, tgs)
            if k == 0:
                if h + 3 < NH:
                    emit_prep2b_load(h + 3)
                emit_prep2b_piece(h)
                if h >= 2:
                    emit_prep2b_fold(h - 2)
            if h >= 2:
                tq3 = emit_ln3_stat(h - 2)
        tq3 = emit_ln3_stat(NH - 2)
        tq3 = emit_ln3_stat(NH - 1)
        prev_rs3 = tok_yev.get((k - 1, ND - 1)) if k >= 1 else None
        if k == 0:
            emit_prep2b_tail()
        emit_wp_load(k, 0)
        emit_wp_load(k, 1)
        emit_y_group(k, 0)
        emit_y_group(k, 1)
        tok_bc3 = rows_and_bcast(tq3, 3, rs3_t, rm3_t, prev_read=prev_rs3)
        tok_bc3_of[k] = tok_bc3
        setfree(("lnbf",), tu)
        if k + 1 < nsup:
            emit_ln1(k + 1)

    j = nsup - 1
    emit_y_evac(j, 0)
    for d in range(2, ND):
        emit_y_group(j, d)
        emit_y_evac(j, d - 1)
    emit_y_evac(j, ND - 1)
    if debug:
        DBG = {}
        for nm, t_ in (("d_carry", carry), ("d_lnbf", ln_bf), ("d_g", g_sb), ("d_thr", thr),
                       ("d_cpp", c_pp), ("d_wbar", wbar_pp), ("d_clam", clam05),
                       ("d_rs3", rs3_t), ("d_rm3", rm3_t), ("d_wfcq", wfc_q),
                       ("d_ta", t_a), ("d_tb", t_b)):
            sh = [t_.shape[0], t_.shape[1]]
            dt_ = t_.dtype
            Dd = nc.dram_tensor(nm, sh, dt_, kind="ExternalOutput")
            b.wait("gpsimd", tok_yev[(j, ND - 1)])
            DBG[nm] = b.emit("gpsimd", lambda E, Dd=Dd, t_=t_: E.dma_start(Dd[:, :], t_[:, :]), inc="d_y", n=16)
        b.wait("sync", list(DBG.values())[-1])
    b.wait("gpsimd", tok_yev[(j, ND - 1)])
    sj, tj = j // 4, (j % 4) * T
    src = xping[j % 2][:, :].rearrange("p (c t) -> p c t", c=ND)
    dst = Yd[sj, :, tj:tj + T].rearrange("(c p) t -> p c t", p=128)
    tok_ystore[j] = b.emit("gpsimd", lambda E, d=dst, s=src: E.dma_start(d, s), inc="d_y", n=16)
    b.wait("sync", ("d_y", b.semc["d_y"]))

    with nc.Block() as blk:
        @blk.sync
        def _(E):
            for f in b.prog["sync"]:
                f(E)

        @blk.tensor
        def _(E):
            for f in b.prog["tensor"]:
                f(E)

        @blk.vector
        def _(E):
            for f in b.prog["vector"]:
                f(E)

        @blk.scalar
        def _(E):
            for f in b.prog["scalar"]:
                f(E)

        @blk.gpsimd
        def _(E):
            for f in b.prog["gpsimd"]:
                f(E)

    es.close()
    return nc


_NC_CACHE = None


def kernel(L0_attn_x, L0_attn_W_r, L0_attn_W_i, L0_attn_lambd,
           L0_ffn_W_fc, L0_ffn_ln_w, L0_ffn_ln_b, L0_ffn_W_proj):
    global _NC_CACHE
    x = np.ascontiguousarray(np.asarray(L0_attn_x), dtype=np.float32)
    xt = np.ascontiguousarray(x.transpose(0, 2, 1))
    WrT = np.ascontiguousarray(np.asarray(L0_attn_W_r, dtype=np.float32).T)
    WiT = np.ascontiguousarray(np.asarray(L0_attn_W_i, dtype=np.float32).T)
    WfcT = np.ascontiguousarray(np.asarray(L0_ffn_W_fc, dtype=np.float32).T)
    WpT = np.ascontiguousarray(np.asarray(L0_ffn_W_proj, dtype=np.float32).T)
    lam = np.ascontiguousarray(np.asarray(L0_attn_lambd, dtype=np.float32).reshape(ND, 128).T)
    lnw = np.ascontiguousarray(np.asarray(L0_ffn_ln_w, dtype=np.float32).reshape(NH, 128).T)
    lnb = np.ascontiguousarray(np.asarray(L0_ffn_ln_b, dtype=np.float32).reshape(NH, 128).T)

    if _NC_CACHE is None:
        _NC_CACHE = build_nc()
    nc = _NC_CACHE

    in_maps = []
    for c in range(8):
        in_maps.append({
            "x": np.ascontiguousarray(xt[2 * c:2 * c + 2]),
            "wrt": WrT, "wit": WiT, "wfct": WfcT, "wpt": WpT,
            "lam": lam, "lnw": lnw, "lnb": lnb,
        })
    res = run_bass_kernel_spmd(nc, in_maps, core_ids=list(range(8)))
    y = np.concatenate([np.asarray(res.results[c]["y"]) for c in range(8)], axis=0)
    out = np.ascontiguousarray(y.transpose(0, 2, 1), dtype=np.float32)
    return out


# version 28
# speedup vs baseline: 1.0043x; 1.0043x over previous
import numpy as np
import ml_dtypes
from contextlib import ExitStack
import concourse.bass as bass
import concourse.mybir as mybir
from concourse.bass_utils import run_bass_kernel_spmd

F32 = mybir.dt.float32
BF16 = mybir.dt.bfloat16
AF = mybir.ActivationFunctionType
ALU = mybir.AluOpType

D = 1024
ND = 8
H = 4096
NH = 32
T = 512
NSUP = 8
SEQ_T = 2048
NSEQ = 2
EPS = 1e-5
QEPS = 1e-6

ENGS = ("sync", "tensor", "vector", "scalar", "gpsimd")
DPERM = (0, 1, 6, 2, 3, 7, 4, 5)


class B:

    def __init__(self, nc, es):
        self.nc = nc
        self.es = es
        self.prog = {e: [] for e in ENGS}
        self.semh = {}
        self.semc = {}
        self.maxw = {e: {} for e in ENGS}

    def sem(self, name):
        self.semh[name] = self.es.enter_context(self.nc.semaphore(name))
        self.semc[name] = 0

    def emit(self, eng, fn, inc=None, n=1):
        if inc is not None:
            if inc.startswith("d_") and self.semc[inc] > 0:
                self.wait(eng, (inc, self.semc[inc]))
            self.semc[inc] += n
            tok = self.semc[inc]
            h = self.semh[inc]
            self.prog[eng].append(lambda E, fn=fn, h=h, n=n: fn(E).then_inc(h, n))
            return (inc, tok)
        self.prog[eng].append(fn)
        return None

    def wait(self, eng, tok):
        if tok is None:
            return
        name, v = tok
        assert v <= self.semc[name], f"wait on future token {name}={v}"
        if self.maxw[eng].get(name, 0) >= v:
            return
        self.maxw[eng][name] = v
        h = self.semh[name]
        self.prog[eng].append(lambda E, h=h, v=v: E.wait_ge(h, v))


def build_nc(nsup=NSUP, debug=False):
    nc = bass.Bass("TRN2", target_bir_lowering=False, detect_race_conditions=True)
    es = ExitStack()
    b = B(nc, es)
    for s in ("pe", "v", "s", "g", "d_tiny", "d_x0", "d_x1", "d_sc0", "d_sc1", "d_sc2", "d_sc3",
              "d_cw", "d_wr0", "d_wr1", "d_wi0", "d_wi1", "d_wp0", "d_wp1", "d_wp2",
              "d_stg0", "d_stg1", "d_stg2", "d_stg3", "d_y"):
        b.sem(s)

    Xd = nc.dram_tensor("x", [NSEQ, D, SEQ_T], F32, kind="ExternalInput")
    WRT = nc.dram_tensor("wrt", [D, D], F32, kind="ExternalInput")
    WIT = nc.dram_tensor("wit", [D, D], F32, kind="ExternalInput")
    WFCT = nc.dram_tensor("wfct", [D, H], F32, kind="ExternalInput")
    WPT = nc.dram_tensor("wpt", [H, D], F32, kind="ExternalInput")
    LAM = nc.dram_tensor("lam", [128, ND], F32, kind="ExternalInput")
    LNW = nc.dram_tensor("lnw", [128, NH], F32, kind="ExternalInput")
    LNB = nc.dram_tensor("lnb", [128, NH], F32, kind="ExternalInput")
    Yd = nc.dram_tensor("y", [NSEQ, D, SEQ_T], F32, kind="ExternalOutput")
    wr_sc = nc.dram_tensor("wr_sc", [ND, 128, D], BF16, kind="Internal")
    wi_sc = nc.dram_tensor("wi_sc", [ND, 128, D], BF16, kind="Internal")
    wp_sc = nc.dram_tensor("wp_sc", [ND, 128, H], BF16, kind="Internal")
    row_sc = nc.dram_tensor("row_sc", [2, D], F32, kind="Internal")

    sb = lambda name, shape, dt: es.enter_context(nc.sbuf_tensor(name, shape, dt))
    wfc_q = sb("wfc_q", [128, ND * H], BF16)
    wr_sl = sb("wr_sl", [128, 2 * D], BF16)
    wi_sl = sb("wi_sl", [128, 2 * D], BF16)
    wp_sl = sb("wp_sl", [128, 3 * 2048], BF16)
    xb0 = sb("xb0", [128, ND * T], F32)
    xb1 = sb("xb1", [128, ND * T], F32)
    ln_bf = sb("ln_bf", [128, ND * T], BF16)
    xbr = sb("xbr", [128, 2 * T], BF16)
    sqr = sb("sqr", [128, 2 * T], BF16)
    t_r = sb("t_r", [128, T], F32)
    t_a = sb("t_a", [128, 2 * T], F32)
    t_i = sb("t_i", [128, 2 * T], F32)
    t_q = sb("t_q", [128, 2 * T], F32)
    t_b = sb("t_b", [128, 2 * T], F32)
    rs_t = sb("rs_t", [128, T], F32)
    mb_t = sb("mb_t", [128, T], F32)
    rs3_t = sb("rs3_t", [128, T], F32)
    rm3_t = sb("rm3_t", [128, T], F32)
    g_sb = sb("g_sb", [128, NH * T], BF16)
    gsq = sb("gsq", [128, 3 * T], BF16)
    tmp = sb("tmp", [128, T], F32)
    rows = sb("rows", [1, 3 * T], F32)
    lam_pp = sb("lam_pp", [128, ND], F32)
    clam05 = sb("clam05", [128, ND], F32)
    lnw_pp = sb("lnw_pp", [128, NH], F32)
    lnb_pp = sb("lnb_pp", [128, NH], F32)
    lnb_bf = sb("lnb_bf", [128, NH], BF16)
    c_pp = sb("c_pp", [128, ND], F32)
    wbar_pp = sb("wbar_pp", [128, ND], F32)
    carry = sb("carry", [128, ND], F32)
    ones_bf = sb("ones_bf", [128, 128], BF16)
    ones_f = sb("ones_f", [128, 128], F32)
    ones_kf = sb("ones_kf", [1, 128], F32)
    accf = sb("accf", [128, 32], F32)
    accp = sb("accp", [128, 32], F32)
    thr = sb("thr", [128, 6], F32)
    cst = sb("cst", [128, 8], F32)
    qn_buf = sb("qn_buf", [128, 4 * 1024], BF16)
    lnw_bf = sb("lnw_bf", [128, NH], BF16)
    lnmc_pp = sb("lnmc_pp", [128, ND], F32)
    lnbw_f = sb("lnbw_f", [128, 2 * NH], F32)
    lnbw_bf = sb("lnbw_bf", [128, 2 * NH], BF16)

    ps = lambda name: es.enter_context(nc.psum_tensor(name, [128, T], F32))
    pool = [ps(f"pool{j}") for j in range(4)]
    pY = [ps("pY0"), ps("pY1")]
    pS = ps("pS")
    pQ = ps("pQ")

    stg0 = xb1[:, 0:1024]
    stg1 = xb1[:, 1024:2048]
    xstg = [xb1[:, j * 1024:(j + 1) * 1024] for j in range(4)]
    s1 = rows[0:1, 0:T]
    s2 = rows[0:1, T:2 * T]
    s3 = rows[0:1, 2 * T:3 * T]
    xping = [xb0, xb1]

    free = {}

    def usewait(eng, key):
        if key in free:
            t = free[key]
            for tt in (t if isinstance(t, list) else [t]):
                b.wait(eng, tt)

    def setfree(key, tok):
        free[key] = tok

    xch = lambda buf, c: buf[:, c * T:(c + 1) * T]
    lnch = lambda c: ln_bf[:, c * T:(c + 1) * T]
    gch = lambda h: g_sb[:, h * T:(h + 1) * T]

    tok_x = {}
    b.emit("sync", lambda E: E.dma_start(lam_pp[:, :], LAM[:, :]), inc="d_tiny", n=16)
    b.emit("sync", lambda E: E.dma_start(lnw_pp[:, :], LNW[:, :]), inc="d_tiny", n=16)
    t_tiny = b.emit("sync", lambda E: E.dma_start(lnb_pp[:, :], LNB[:, :]), inc="d_tiny", n=16)
    for k in (0,):
        seq, t0 = k // 4, (k % 4) * T
        dst = xping[k % 2][:, :].rearrange("p (c t) -> p c t", c=ND)
        src = Xd[seq, :, t0:t0 + T].rearrange("(c p) t -> p c t", p=128)
        tok_x[k] = b.emit("gpsimd", lambda E, d=dst, s=src: E.dma_start(d, s), inc=f"d_x{k % 2}", n=16)

    b.emit("vector", lambda E: E.memset(cst[:, 0:1], 0.5))
    b.emit("vector", lambda E: E.memset(cst[:, 1:2], -0.5))
    b.emit("vector", lambda E: E.memset(cst[:, 2:3], 1.0 / 1024.0))
    b.emit("vector", lambda E: E.memset(cst[:, 3:4], 1.0 / 4096.0))
    b.emit("vector", lambda E: E.memset(cst[:, 4:5], EPS))
    b.emit("vector", lambda E: E.memset(cst[:, 5:6], 4.0))
    b.emit("vector", lambda E: E.memset(cst[:, 6:7], -8.0))
    b.emit("vector", lambda E: E.memset(ones_bf[:, :], 1.0))
    b.emit("vector", lambda E: E.memset(ones_f[:, :], 1.0))
    tok_ones = b.emit("vector", lambda E: E.memset(ones_kf[:, :], 1.0), inc="v")
    b.wait("vector", t_tiny)
    b.emit("vector", lambda E: E.tensor_copy(lnb_bf[:, :], lnb_pp[:, :]))
    tok_lnbbf = b.emit("vector", lambda E: E.tensor_copy(lnw_bf[:, :], lnw_pp[:, :]), inc="v")
    lnbw_v = lnbw_f.rearrange("p (c two) -> p c two", two=2)
    b.emit("vector", lambda E: E.tensor_copy(lnbw_v[:, :, 0], lnb_pp[:, :]))
    tok_lnbw0 = b.emit("vector", lambda E: E.tensor_copy(lnbw_v[:, :, 1], lnw_pp[:, :]), inc="v")
    b.wait("vector", tok_lnbw0)
    tok_lnbw = b.emit("vector", lambda E: E.tensor_copy(lnbw_bf[:, :], lnbw_f[:, :]), inc="v")
    b.wait("scalar", t_tiny)
    b.wait("scalar", tok_ones)
    tA = b.emit("scalar", lambda E: E.activation(clam05[:, :], lam_pp[:, :], AF.Exp, scale=-1.0), inc="s")
    b.wait("scalar", tA)
    tA = b.emit("scalar", lambda E: E.activation(clam05[:, :], clam05[:, :], AF.Ln, bias=1.0), inc="s")
    b.wait("scalar", tA)
    tA = b.emit("scalar", lambda E: E.activation(clam05[:, :], clam05[:, :], AF.Copy, scale=cst[:, 6:7]), inc="s")
    b.wait("scalar", tA)
    tok_clam = b.emit("scalar", lambda E: E.activation(lnmc_pp[:, :], clam05[:, :], AF.Ln, scale=-1.0), inc="s")

    qn_s = lambda i: qn_buf[:, (i % 4) * 1024:((i % 4) + 1) * 1024]
    sc_last = {}

    def emit_scatter(slot, dst, src_ap):
        sem = f"d_sc{slot % 4}"
        qeng = "sync" if slot % 2 == 0 else "gpsimd"
        tw = b.emit(qeng, lambda E, d=dst, s=src_ap: E.dma_start(d, s), inc=sem, n=16)
        setfree(("qn4", slot % 4), tw)
        sc_last[sem] = tw
        return tw

    for widx, (WSRC, WSC) in enumerate(((WRT, wr_sc), (WIT, wi_sc))):
        for kd in range(ND):
            slot = (widx * ND + kd) % 4
            st = stg0 if kd % 2 == 0 else stg1
            qeng = "sync" if kd % 2 == 0 else "gpsimd"
            usewait(qeng, ("stg", kd % 2))
            tl = b.emit(qeng, lambda E, st=st, W=WSRC, kd=kd:
                        E.dma_start(st, W[128 * kd:128 * (kd + 1), :]), inc=f"d_stg{kd % 2}", n=16)
            b.wait("vector", tl)
            usewait("vector", ("qn4", slot))
            tc = b.emit("vector", lambda E, sl=qn_s(slot), st=st: E.tensor_copy(sl, st), inc="v")
            setfree(("stg", kd % 2), tc)
            qw = "sync" if slot % 2 == 0 else "gpsimd"
            b.wait(qw, tc)
            dst = WSC[:, :, kd * 128:(kd + 1) * 128].transpose([1, 0, 2])
            src2 = qn_s(slot).rearrange("p (ke j) -> p ke j", ke=ND)
            emit_scatter(slot, dst, src2)
    tok_wrsc = list(sc_last.values())
    tok_wisc = []

    tstg = [t_a[:, :], t_q[:, :], t_b[:, :], t_i[:, :]]
    tkeymap = ["t_a", "t_q", "t_b", "t_i"]
    gfc_state = {"last": None}
    tok_wfc_pc = {}
    p2b_defer = {}

    gfc_toks = {}

    def emit_gfc_loads(qi):
        for j in range(4):
            i = 4 * qi + j
            pc, kd = i // 8, i % 8
            sl = i % 4
            st = xstg[sl]
            qeng = "sync" if sl % 2 == 0 else "gpsimd"
            usewait(qeng, ("stg", sl))
            gfc_toks[i] = b.emit(qeng, lambda E, st=st, kd=kd, pc=pc:
                                 E.dma_start(st, WFCT[128 * kd:128 * (kd + 1), 1024 * pc:1024 * (pc + 1)]),
                                 inc=f"d_stg{sl}", n=16)

    def emit_gfc_abs(qi):
        for j in range(4):
            i = 4 * qi + j
            sl = i % 4
            st = xstg[sl]
            b.wait("scalar", gfc_toks[i])
            red = b.emit("scalar", lambda E, st=st, i=i:
                         E.activation(st, st, AF.Abs, accum_out=accf[:, i:i + 1]), inc="s")
            setfree(("stg", sl), red)
            gfc_state["last"] = red

    def emit_thr(acc, pool_bank, col):
        usewait("tensor", ("pool", pool_bank))
        tg = b.emit("tensor", lambda E: E.matmul(pool[pool_bank][:, 0:32], ones_f[:, :], acc[:, 0:32],
                                                 start=True, stop=True), inc="pe")
        b.wait("vector", tg)
        tA = b.emit("vector", lambda E: E.tensor_reduce(thr[:, col:col + 1], pool[pool_bank][:, 0:32],
                                                        mybir.AxisListType.X, ALU.add), inc="v")
        setfree(("pool", pool_bank), tA)
        b.wait("vector", tA)
        tA = b.emit("vector", lambda E: E.tensor_scalar(thr[:, col:col + 1], thr[:, col:col + 1],
                                                        1.0 / (H * D), QEPS, ALU.mult, ALU.add), inc="v")
        b.wait("vector", tA)
        b.emit("vector", lambda E: E.tensor_scalar_mul(thr[:, col + 1:col + 2], thr[:, col:col + 1], 0.5))
        return b.emit("vector", lambda E: E.tensor_scalar_mul(thr[:, col + 2:col + 3], thr[:, col:col + 1], -0.5), inc="v")

    def emit_ln2_tile_one(c, xb, tok_bc2, tok_i7, tok_gbl):
        eng, scr, inc, key = (("vector", tmp, "v", ("tmp",)) if c < 6 else
                              ("gpsimd", t_r, "g", ("t_r",)))
        b.wait(eng, tok_bc2)
        b.wait(eng, tok_i7)
        b.wait(eng, tok_gbl)
        usewait(eng, ("lnbf",))
        usewait(eng, key)
        tA = b.emit(eng, lambda E, c=c, xb=xb, scr=scr: E.tensor_tensor(scr[:, :], xch(xb, c), rs_t[:, :], ALU.mult), inc=inc)
        b.wait(eng, tA)
        tln2 = b.emit(eng, lambda E, c=c, scr=scr: E.tensor_tensor(lnch(c), scr[:, :], mb_t[:, :], ALU.add), inc=inc)
        setfree(key, tln2)
        return tln2

    def emit_prep2a_mid(xb, tok_bc2, tok_i7, tok_gbl):
        nonlocal tok_wfc, tok_thrp
        ln2_toks = {}
        b.wait("tensor", gfc_state["last"])
        b.wait("tensor", tok_ones)
        tok_thrf = emit_thr(accf, 0, 0)
        red_p = None
        for i in range(32):
            pc, kd = i // 8, i % 8
            sl = i % 4
            st = xstg[sl]
            qeng = "sync" if sl % 2 == 0 else "gpsimd"
            usewait(qeng, ("stg", sl))
            tl = b.emit(qeng, lambda E, st=st, kd=kd, pc=pc:
                        E.dma_start(st, WFCT[128 * kd:128 * (kd + 1), 1024 * pc:1024 * (pc + 1)]),
                        inc=f"d_stg{sl}", n=16)
            st2 = tstg[sl]
            qeng2 = "sync" if sl % 2 == 0 else "gpsimd"
            if i < 4:
                usewait(qeng2, (tkeymap[sl], 0))
                usewait(qeng2, (tkeymap[sl], 1))
            else:
                usewait(qeng2, ("tstg", sl))
            tl2 = b.emit(qeng2, lambda E, st2=st2, i=i:
                         E.dma_start(st2, WPT[128 * i:128 * (i + 1), :]), inc=f"d_stg{sl}", n=16)
            b.wait("vector", tl)
            b.wait("vector", tok_thrf)
            usewait("vector", ("qn4", sl))
            tA2 = b.emit("vector", lambda E, st=st, sl=sl:
                         E.tensor_single_scalar(qn_s(sl), st, thr[:, 2:3], ALU.is_le), inc="v")
            b.wait("vector", tA2)
            dstq = wfc_q[:, kd * H + pc * 1024: kd * H + (pc + 1) * 1024]
            tq = b.emit("vector", lambda E, st=st, d=dstq, sl=sl:
                        E.scalar_tensor_tensor(d, st, thr[:, 1:2], qn_s(sl),
                                               ALU.is_ge, ALU.subtract), inc="v")
            setfree(("qn4", sl), tq)
            setfree(("stg", sl), tq)
            if kd == 7:
                tok_wfc_pc[pc] = tq
            b.wait("scalar", tl2)
            red_p = b.emit("scalar", lambda E, st2=st2, i=i:
                           E.activation(st2, st2, AF.Abs, accum_out=accp[:, i:i + 1]), inc="s")
            setfree(("tstg", sl), red_p)
            if i % 2 == 0 and i // 2 < ND:
                ln2_toks[i // 2] = emit_ln2_tile_one(i // 2, xb, tok_bc2, tok_i7, tok_gbl)
        tok_wfc = tok_wfc_pc[3]
        b.wait("tensor", red_p)
        tok_thrp = emit_thr(accp, 1, 3)
        for sl in range(4):
            usewait("gpsimd", ("stg", sl))
        dst1 = xping[1][:, :].rearrange("p (c t) -> p c t", c=ND)
        src1 = Xd[0, :, T:2 * T].rearrange("(c p) t -> p c t", p=128)
        tok_x[1] = b.emit("gpsimd", lambda E, d=dst1, s=src1: E.dma_start(d, s), inc="d_x1", n=16)
        return [ln2_toks[c] for c in range(ND)]

    p2b_loads = {}

    def emit_prep2b_load(i):
        sl = i % 4
        st = tstg[sl]
        qeng = "sync" if sl % 2 == 0 else "gpsimd"
        usewait(qeng, ("tstg", sl))
        p2b_loads[i] = b.emit(qeng, lambda E, st=st, i=i:
                              E.dma_start(st, WPT[128 * i:128 * (i + 1), :]), inc=f"d_stg{sl}", n=16)

    def emit_prep2b_piece(i):
        nonlocal tok_cw
        sl = i % 4
        st = tstg[sl]
        tl = p2b_loads.pop(i)
        b.wait("vector", tl)
        b.wait("vector", tok_thrp)
        usewait("vector", ("qn4", sl))
        tA4 = b.emit("vector", lambda E, st=st, sl=sl:
                     E.tensor_single_scalar(qn_s(sl), st, thr[:, 5:6], ALU.is_le), inc="v")
        b.wait("vector", tA4)
        tq1 = b.emit("vector", lambda E, st=st, sl=sl:
                     E.scalar_tensor_tensor(qn_s(sl), st, thr[:, 4:5], qn_s(sl),
                                            ALU.is_ge, ALU.subtract), inc="v")
        setfree(("tstg", sl), tq1)
        b.wait("tensor", tq1)
        b.wait("tensor", tok_lnbw)
        qh = qn_s(sl)
        b.emit("tensor", lambda E, i=i, qh=qh: E.matmul(pY[0][0:2, :], lnbw_bf[:, 2 * i:2 * i + 2],
                                                        qh[:, 0:512], start=(i == 0), stop=(i == 31)))
        tw2 = b.emit("tensor", lambda E, i=i, qh=qh: E.matmul(pY[1][0:2, :], lnbw_bf[:, 2 * i:2 * i + 2],
                                                              qh[:, 512:1024], start=(i == 0), stop=(i == 31)),
                     inc="pe")
        if i == 31:
            tok_cw = tw2
        p2b_defer[i] = (qh, tw2)

    def emit_prep2b_fold(i):
        qh, tw2 = p2b_defer.pop(i)
        b.wait("vector", tw2)
        tq2 = b.emit("vector", lambda E, i=i, qh=qh:
                     E.tensor_scalar_mul(qh, qh, lnw_pp[:, i:i + 1]), inc="v")
        qw = "sync" if i % 2 == 0 else "gpsimd"
        b.wait(qw, tq2)
        dst = wp_sc[:, :, i * 128:(i + 1) * 128].transpose([1, 0, 2])
        src2 = qh.rearrange("p (kd j) -> p kd j", kd=ND)
        emit_scatter(i, dst, src2)

    def emit_prep2b_tail():
        nonlocal tok_cwe, tok_cpp, tok_wpsc
        emit_prep2b_fold(NH - 2)
        emit_prep2b_fold(NH - 1)
        tok_wpsc = [sc_last[s] for s in ("d_sc0", "d_sc1", "d_sc2", "d_sc3")]
        for sl in range(4):
            tok = free[("tstg", sl)]
            setfree((tkeymap[sl], 0), tok)
            setfree((tkeymap[sl], 1), tok)
        b.wait("vector", tok_cw)
        usewait("vector", ("tmp",))
        b.emit("vector", lambda E: E.tensor_copy(tmp[0:2, :], pY[0][0:2, :]))
        tok_cwe = b.emit("vector", lambda E: E.tensor_copy(tmp[32:34, :], pY[1][0:2, :]), inc="v")
        setfree(("pY", 0), tok_cwe)
        setfree(("pY", 1), tok_cwe)
        b.wait("sync", tok_cwe)
        b.emit("sync", lambda E: E.dma_start(row_sc[0:1, 0:512], tmp[0:1, :]), inc="d_cw", n=16)
        b.emit("sync", lambda E: E.dma_start(row_sc[0:1, 512:1024], tmp[32:33, :]), inc="d_cw", n=16)
        b.emit("sync", lambda E: E.dma_start(row_sc[1:2, 0:512], tmp[1:2, :]), inc="d_cw", n=16)
        trw = b.emit("sync", lambda E: E.dma_start(row_sc[1:2, 512:1024], tmp[33:34, :]), inc="d_cw", n=16)
        b.wait("sync", trw)
        b.emit("sync", lambda E: E.dma_start(c_pp[:, :], row_sc[0, :].rearrange("(c p) -> p c", p=128),
                                             allow_slow_non_contiguous=True), inc="d_cw", n=16)
        tok_cpp = b.emit("sync", lambda E: E.dma_start(wbar_pp[:, :], row_sc[1, :].rearrange("(c p) -> p c", p=128),
                                                       allow_slow_non_contiguous=True), inc="d_cw", n=16)
        setfree(("tmp",), tok_cpp)

    tok_wfc = tok_thrp = tok_cw = tok_cwe = tok_cpp = None
    tok_wpsc = []

    def ln_stats_mm(xbuf):
        for c in range(ND):
            sl = c % 2
            usewait("vector", ("xbr", sl))
            tA = b.emit("vector", lambda E, c=c, sl=sl, xb=xbuf: E.tensor_copy(xbr[:, sl * T:(sl + 1) * T], xch(xb, c)), inc="v")
            b.wait("vector", tA)
            tsq = b.emit("vector", lambda E, sl=sl:
                         E.tensor_tensor(sqr[:, sl * T:(sl + 1) * T], xbr[:, sl * T:(sl + 1) * T],
                                         xbr[:, sl * T:(sl + 1) * T], ALU.mult), inc="v")
            b.wait("tensor", tsq)
            if c == 0:
                usewait("tensor", ("pS",))
                usewait("tensor", ("pQ",))
            b.emit("tensor", lambda E, c=c, sl=sl: E.matmul(pS[:, :], ones_bf[:, :], xbr[:, sl * T:(sl + 1) * T],
                                                            start=(c == 0), stop=(c == ND - 1)))
            tst = b.emit("tensor", lambda E, c=c, sl=sl: E.matmul(pQ[:, :], ones_bf[:, :], sqr[:, sl * T:(sl + 1) * T],
                                                                  start=(c == 0), stop=(c == ND - 1)), inc="pe")
            setfree(("xbr", sl), tst)
        return tst

    def rows_and_bcast(tstats, divcol, RS, MB, prev_read=None, prev_read_g=None):
        b.wait("scalar", tstats)
        usewait("scalar", ("rows",))
        b.emit("scalar", lambda E: E.activation(s1, pS[0:1, :], AF.Copy, scale=cst[0:1, divcol:divcol + 1]))
        trow = b.emit("scalar", lambda E: E.activation(s2, pQ[0:1, :], AF.Copy, scale=cst[0:1, divcol:divcol + 1]), inc="s")
        b.wait("vector", trow)
        tA = b.emit("vector", lambda E: E.tensor_tensor(s3, s1, s1, ALU.mult), inc="v")
        b.wait("vector", tA)
        tv1 = b.emit("vector", lambda E: E.scalar_tensor_tensor(s3, s3, -1.0, s2, ALU.mult, ALU.add), inc="v")
        b.wait("scalar", tv1)
        tA = b.emit("scalar", lambda E: E.activation(s3, s3, AF.Ln, bias=cst[0:1, 4:5]), inc="s")
        b.wait("scalar", tA)
        ts2 = b.emit("scalar", lambda E: E.activation(s2, s3, AF.Exp, scale=cst[0:1, 1:2]), inc="s")
        b.wait("vector", ts2)
        trow2 = b.emit("vector", lambda E: E.scalar_tensor_tensor(s1, s1, -1.0, s2, ALU.mult, ALU.mult), inc="v")
        b.wait("tensor", trow2)
        tb1 = b.emit("tensor", lambda E: E.matmul(pS[:, :], ones_kf[:, :], s2, start=True, stop=True), inc="pe")
        tb2 = b.emit("tensor", lambda E: E.matmul(pQ[:, :], ones_kf[:, :], s1, start=True, stop=True), inc="pe")
        setfree(("rows",), tb2)
        b.wait("vector", tb1)
        if prev_read is not None:
            b.wait("vector", prev_read)
        if prev_read_g is not None:
            b.wait("vector", prev_read_g)
        b.emit("vector", lambda E, R=RS: E.tensor_copy(R[:, :], pS[:, :]), inc="v")
        b.wait("vector", tb2)
        tbc = b.emit("vector", lambda E, M=MB: E.tensor_copy(M[:, :], pQ[:, :]), inc="v")
        setfree(("pS",), tbc)
        setfree(("pQ",), tbc)
        return tbc

    def emit_y_group(j, d):
        bank = d % 2
        usewait("tensor", ("pY", bank))
        tok = None
        for half in range(2):
            hh = d * 2 + half
            b.wait("tensor", tok_wp[(j, hh)])
            for kk in range(16):
                kh = half * 16 + kk
                sl = hh % 3
                last = (half == 1 and kk == 15)
                fn = lambda E, kh=kh, sl=sl, bank=bank, first=(kh == 0), last=last: E.matmul(
                    pY[bank][:, :], wp_sl[:, sl * 2048 + (kh % 16) * 128: sl * 2048 + (kh % 16) * 128 + 128],
                    gch(kh), start=first, stop=last)
                if kk == 15:
                    tok = b.emit("tensor", fn, inc="pe")
                    setfree(("wpsl", sl), tok)
                else:
                    b.emit("tensor", fn)
        tok_ymm[(j, d)] = tok
        for hh2 in (2 * d + 2, 2 * d + 3):
            if hh2 < 2 * ND and (j, hh2) not in tok_wp:
                emit_wp_load(j, hh2)

    def emit_y_evac(j, d):
        bank = d % 2
        xb = xping[j % 2]
        b.wait("vector", tok_ymm[(j, d)])
        b.wait("vector", tok_bc3_of[j])
        b.wait("vector", tok_cpp)
        usewait("vector", ("tmp",))
        tA = b.emit("vector", lambda E, bank=bank: E.tensor_tensor(tmp[:, :], pY[bank][:, :], rs3_t[:, :], ALU.mult), inc="v")
        b.wait("vector", tA)
        tfree = b.emit("vector", lambda E, d=d: E.scalar_tensor_tensor(
            tmp[:, :], rm3_t[:, :], wbar_pp[:, d:d + 1], tmp[:, :], ALU.mult, ALU.add), inc="v")
        setfree(("pY", bank), tfree)
        b.wait("vector", tfree)
        tok = b.emit("vector", lambda E, d=d, xb=xb: E.scalar_tensor_tensor(
            xch(xb, d), tmp[:, :], c_pp[:, d:d + 1], xch(xb, d), ALU.add, ALU.add), inc="v")
        setfree(("tmp",), tok)
        tok_yev[(j, d)] = tok

    def emit_wri_load(k, e):
        sl = (k * ND + e) % 2
        usewait("sync", ("wrsl", sl))
        for t_ in tok_wrsc:
            b.wait("sync", t_)
        tok_wr[(k, e, 0)] = b.emit("sync", lambda E, e=e, sl=sl:
                                   E.dma_start(wr_sl[:, sl * D:(sl + 1) * D], wr_sc[e, :, :]), inc=f"d_wr{sl}", n=16)
        usewait("sync", ("wisl", sl))
        for t_ in tok_wisc:
            b.wait("sync", t_)
        tok_wr[(k, e, 1)] = b.emit("sync", lambda E, e=e, sl=sl:
                                   E.dma_start(wi_sl[:, sl * D:(sl + 1) * D], wi_sc[e, :, :]), inc=f"d_wi{sl}", n=16)

    def emit_wp_load(j, hh):
        d, half = hh // 2, hh % 2
        sl = hh % 3
        usewait("sync", ("wpsl", sl))
        for t_ in tok_wpsc:
            b.wait("sync", t_)
        tok_wp[(j, hh)] = b.emit("sync", lambda E, d=d, half=half, sl=sl:
                                 E.dma_start(wp_sl[:, sl * 2048:(sl + 1) * 2048],
                                             wp_sc[d, :, half * 2048:(half + 1) * 2048]), inc=f"d_wp{sl}", n=16)

    tok_wr = {}
    tok_wp = {}
    tok_ri = {}
    tok_ymm = {}
    tok_yev = {}
    tok_ystore = {}
    tok_gelu = {}

    tok_bc3_of = {}
    tok_ln1_of = {}
    tok_ln2_of = {}

    tok_x1_of = {}

    def emit_ln2_stat_chunk(k, xbuf, c):
        sl = c % 2
        eng, inc = ("gpsimd", "g")
        b.wait(eng, tok_x1_of[c])
        usewait(eng, ("xbr", sl))
        tA = b.emit(eng, lambda E, c=c, sl=sl, xb=xbuf: E.tensor_copy(xbr[:, sl * T:(sl + 1) * T], xch(xb, c)), inc=inc)
        b.wait(eng, tA)
        tsq = b.emit(eng, lambda E, sl=sl:
                     E.tensor_tensor(sqr[:, sl * T:(sl + 1) * T], xbr[:, sl * T:(sl + 1) * T],
                                     xbr[:, sl * T:(sl + 1) * T], ALU.mult), inc=inc)
        b.wait("tensor", tsq)
        if c == 0:
            usewait("tensor", ("pS",))
            usewait("tensor", ("pQ",))
        b.emit("tensor", lambda E, c=c, sl=sl: E.matmul(pS[:, :], ones_bf[:, :], xbr[:, sl * T:(sl + 1) * T],
                                                        start=(c == 0), stop=(c == ND - 1)))
        tst = b.emit("tensor", lambda E, c=c, sl=sl: E.matmul(pQ[:, :], ones_bf[:, :], sqr[:, sl * T:(sl + 1) * T],
                                                              start=(c == 0), stop=(c == ND - 1)), inc="pe")
        setfree(("xbr", sl), tst)
        return tst

    def emit_ln1(k):
        xb = xping[k % 2]
        b.wait("vector", tok_x[k])
        tstat1 = ln_stats_mm(xb)
        pr = tok_ln2_of.get(k - 1)
        tok_bc1 = rows_and_bcast(tstat1, 2, rs_t, mb_t, prev_read=pr[5] if pr else None,
                                 prev_read_g=pr[7] if pr else None)
        lt = []
        for c in range(ND):
            eng, scr, inc, key = (("vector", tmp, "v", ("tmp",)) if c < 6 else
                                  ("gpsimd", t_r, "g", ("t_r",)))
            b.wait(eng, tok_bc1)
            usewait(eng, ("lnbf",))
            usewait(eng, key)
            tA = b.emit(eng, lambda E, c=c, xb=xb, scr=scr: E.tensor_tensor(scr[:, :], xch(xb, c), rs_t[:, :], ALU.mult), inc=inc)
            b.wait(eng, tA)
            tln = b.emit(eng, lambda E, c=c, scr=scr: E.tensor_tensor(lnch(c), scr[:, :], mb_t[:, :], ALU.add), inc=inc)
            setfree(key, tln)
            lt.append(tln)
        tok_ln1_of[k] = lt
        free.pop(("lnbf",), None)

    emit_ln1(0)

    for k in range(nsup):
        xb = xping[k % 2]
        seq, t0 = k // 4, (k % 4) * T
        ln1lt = tok_ln1_of[k]
        tok_ln1, tok_ln1g = ln1lt[5], ln1lt[7]

        for e in (0, 1):
            emit_wri_load(k, e)
        if k == 0:
            emit_gfc_loads(0)

        for e in range(ND):
            q = e % 2
            pR = pool[2 * q]
            pI = pool[2 * q + 1]
            b.wait("tensor", tok_wr[(k, e, 0)])
            usewait("tensor", ("pool", 2 * q))
            sl = (k * ND + e) % 2
            for ii, d in enumerate(DPERM):
                b.wait("tensor", ln1lt[d])
                fn = lambda E, d=d, sl=sl, pR=pR, ii=ii: E.matmul(
                    pR[:, :], wr_sl[:, sl * D + d * 128: sl * D + d * 128 + 128], lnch(d),
                    start=(ii == 0), stop=(ii == ND - 1))
                if ii == ND - 1:
                    tok_r = b.emit("tensor", fn, inc="pe")
                    setfree(("wrsl", sl), tok_r)
                else:
                    b.emit("tensor", fn)
            b.wait("tensor", tok_wr[(k, e, 1)])
            usewait("tensor", ("pool", 2 * q + 1))
            for ii, d in enumerate(DPERM):
                fn = lambda E, d=d, sl=sl, pI=pI, ii=ii: E.matmul(
                    pI[:, :], wi_sl[:, sl * D + d * 128: sl * D + d * 128 + 128], lnch(d),
                    start=(ii == 0), stop=(ii == ND - 1))
                if ii == ND - 1:
                    tok_i = b.emit("tensor", fn, inc="pe")
                    setfree(("wisl", sl), tok_i)
                else:
                    b.emit("tensor", fn)
            tok_ri[(k, e)] = (tok_r, tok_i)
            if e + 2 < ND:
                emit_wri_load(k, e + 2)
            if k >= 1 and 1 <= e <= 7:
                emit_y_evac(k - 1, e - 1)
            if k >= 1 and 1 <= e <= 6:
                emit_y_group(k - 1, e + 1)
            if k == 0:
                emit_gfc_abs(e)
                if e + 1 < ND:
                    emit_gfc_loads(e + 1)

            b.wait("scalar", tok_r)
            b.wait("scalar", tok_clam)
            usewait("scalar", ("t_a", q))
            usewait("scalar", ("t_r",))
            tA = b.emit("scalar", lambda E, pR=pR: E.activation(t_r[:, :], pR[:, :], AF.Exp, scale=-1.0), inc="s")
            setfree(("pool", 2 * q), tA)
            b.wait("scalar", tA)
            tA = b.emit("scalar", lambda E: E.activation(t_r[:, :], t_r[:, :], AF.Ln, bias=1.0), inc="s")
            b.wait("scalar", tA)
            usewait("scalar", ("t_a", q))
            tA = b.emit("scalar", lambda E, q=q, e=e: E.activation(
                t_a[:, q * T:(q + 1) * T], t_r[:, :], AF.Exp, scale=-1.0,
                bias=lnmc_pp[:, e:e + 1]), inc="s")
            setfree(("t_r",), tA)
            b.wait("scalar", tA)
            tfR = b.emit("scalar", lambda E, q=q: E.activation(
                t_a[:, q * T:(q + 1) * T], t_a[:, q * T:(q + 1) * T], AF.Exp, scale=-1.0), inc="s")
            b.wait("scalar", tok_i)
            usewait("scalar", ("t_i", q))
            tA = b.emit("scalar", lambda E, pI=pI, q=q: E.activation(
                t_i[:, q * T:(q + 1) * T], pI[:, :], AF.Exp, scale=-1.0), inc="s")
            setfree(("pool", 2 * q + 1), tA)
            b.wait("scalar", tA)
            tvi = b.emit("scalar", lambda E, q=q: E.activation(
                t_i[:, q * T:(q + 1) * T], t_i[:, q * T:(q + 1) * T], AF.Ln, bias=1.0), inc="s")
            b.wait("vector", tfR)
            usewait("vector", ("t_q", q))
            tv1 = b.emit("vector", lambda E, q=q: E.scalar_tensor_tensor(
                t_q[:, q * T:(q + 1) * T], t_a[:, q * T:(q + 1) * T], -1.0,
                t_a[:, q * T:(q + 1) * T], ALU.mult, ALU.mult), inc="v")
            b.wait("scalar", tv1)
            tA = b.emit("scalar", lambda E, q=q: E.activation(
                t_q[:, q * T:(q + 1) * T], t_q[:, q * T:(q + 1) * T], AF.Ln, bias=1.0), inc="s")
            b.wait("vector", tA)
            b.wait("vector", tvi)
            tvm = b.emit("vector", lambda E, q=q: E.scalar_tensor_tensor(
                t_q[:, q * T:(q + 1) * T], t_q[:, q * T:(q + 1) * T], 0.5,
                t_i[:, q * T:(q + 1) * T], ALU.mult, ALU.subtract), inc="v")
            setfree(("t_i", q), tvm)
            b.wait("scalar", tvm)
            tst = b.emit("scalar", lambda E, q=q: E.activation(
                t_q[:, q * T:(q + 1) * T], t_q[:, q * T:(q + 1) * T], AF.Exp), inc="s")
            b.wait("vector", tst)
            b.wait("vector", ln1lt[e])
            usewait("vector", ("t_b", q))
            tgb = b.emit("vector", lambda E, q=q, e=e: E.tensor_tensor(
                t_b[:, q * T:(q + 1) * T], t_q[:, q * T:(q + 1) * T], lnch(e), ALU.mult), inc="v")
            setfree(("t_q", q), tgb)
            b.wait("vector", tgb)
            init = 0.0 if (k % 4 == 0) else carry[:, e:e + 1]
            tA = b.emit("vector", lambda E, q=q, init=init: E.tensor_tensor_scan(
                t_b[:, q * T:(q + 1) * T], t_a[:, q * T:(q + 1) * T], t_b[:, q * T:(q + 1) * T],
                init, ALU.mult, ALU.add), inc="v")
            b.wait("vector", tA)
            b.emit("vector", lambda E, q=q, e=e: E.tensor_copy(carry[:, e:e + 1], t_b[:, q * T + T - 1: q * T + T]))
            tx1 = b.emit("vector", lambda E, q=q, e=e, xb=xb: E.tensor_tensor(
                xch(xb, e), xch(xb, e), t_b[:, q * T:(q + 1) * T], ALU.add), inc="v")
            setfree(("t_a", q), tx1)
            setfree(("t_b", q), tx1)
            tok_gb_last = tgb
            tok_x1_of[e] = tx1
            if e >= 1:
                emit_ln2_stat_chunk(k, xb, e - 1)

        tstat2 = emit_ln2_stat_chunk(k, xb, ND - 1)
        if k >= 1:
            emit_wp_load(k, 0)
            emit_wp_load(k, 1)
        tok_bc2 = rows_and_bcast(tstat2, 2, rs_t, mb_t, prev_read=tok_ln1, prev_read_g=tok_ln1g)
        if k == 0:
            tok_ln2 = emit_prep2a_mid(xb, tok_bc2, tok_ri[(k, ND - 1)][1], tok_x1_of[ND - 1])
        else:
            lt = [emit_ln2_tile_one(c, xb, tok_bc2, tok_ri[(k, ND - 1)][1], tok_x1_of[ND - 1])
                  for c in range(ND)]
            tok_ln2 = lt
        tok_ln2_of[k] = tok_ln2
        if k >= 1:
            emit_y_evac(k - 1, 7)
            j = k - 1
            b.wait("gpsimd", tok_yev[(j, ND - 1)])
            sj, tj = j // 4, (j % 4) * T
            src = xping[j % 2][:, :].rearrange("p (c t) -> p c t", c=ND)
            dst = Yd[sj, :, tj:tj + T].rearrange("(c p) t -> p c t", p=128)
            tok_ystore[j] = b.emit("gpsimd", lambda E, d=dst, s=src: E.dma_start(d, s), inc="d_y", n=16)
            if k + 1 < nsup:
                kk = k + 1
                b.wait("gpsimd", tok_ystore.get(kk - 2, None))
                sq2, t02 = kk // 4, (kk % 4) * T
                dst2 = xping[kk % 2][:, :].rearrange("p (c t) -> p c t", c=ND)
                src2 = Xd[sq2, :, t02:t02 + T].rearrange("(c p) t -> p c t", p=128)
                tok_x[kk] = b.emit("gpsimd", lambda E, d=dst2, s=src2: E.dma_start(d, s), inc=f"d_x{kk % 2}", n=16)

        if k > 0:
            b.wait("tensor", tok_wfc)
        else:
            for i3 in range(3):
                emit_prep2b_load(i3)
        gs_toks = {}

        def emit_ln3_stat(h):
            tge2, tgs2 = gs_toks[h]
            b.wait("tensor", tge2)
            b.wait("tensor", tgs2)
            if h == 0:
                usewait("tensor", ("pS",))
                usewait("tensor", ("pQ",))
            b.emit("tensor", lambda E, h=h: E.matmul(pS[:, :], ones_bf[:, :], gch(h),
                                                     start=(h == 0), stop=(h == NH - 1)))
            tq = b.emit("tensor", lambda E, h=h: E.matmul(pQ[:, :], ones_bf[:, :],
                                                          gsq[:, (h % 3) * T:(h % 3 + 1) * T],
                                                          start=(h == 0), stop=(h == NH - 1)), inc="pe")
            setfree(("gsq", h % 3), tq)
            return tq

        for h in range(NH):
            bank = h % 4
            if k == 0:
                b.wait("tensor", tok_wfc_pc[h // 8])
            usewait("tensor", ("pool", bank))
            for ii, d in enumerate(DPERM):
                b.wait("tensor", tok_ln2[d])
                fn = lambda E, h=h, d=d, bank=bank, ii=ii: E.matmul(
                    pool[bank][:, :], wfc_q[:, d * H + h * 128: d * H + h * 128 + 128], lnch(d),
                    start=(ii == 0), stop=(ii == ND - 1))
                if ii == ND - 1:
                    tu = b.emit("tensor", fn, inc="pe")
                else:
                    b.emit("tensor", fn)
            b.wait("scalar", tu)
            tge = b.emit("scalar", lambda E, h=h, bank=bank: E.activation(
                gch(h), pool[bank][:, :], AF.Gelu), inc="s")
            setfree(("pool", bank), tge)
            tok_gelu[(k, h)] = tge
            geng, ginc = ("gpsimd", "g") if k == 0 else ("vector", "v")
            b.wait(geng, tge)
            usewait(geng, ("gsq", h % 3))
            tgs = b.emit(geng, lambda E, h=h: E.tensor_tensor(
                gsq[:, (h % 3) * T:(h % 3 + 1) * T], gch(h), gch(h), ALU.mult), inc=ginc)
            gs_toks[h] = (tge, tgs)
            if k == 0:
                if h + 3 < NH:
                    emit_prep2b_load(h + 3)
                emit_prep2b_piece(h)
                if h >= 2:
                    emit_prep2b_fold(h - 2)
            if h >= 2:
                tq3 = emit_ln3_stat(h - 2)
        tq3 = emit_ln3_stat(NH - 2)
        tq3 = emit_ln3_stat(NH - 1)
        prev_rs3 = tok_yev.get((k - 1, ND - 1)) if k >= 1 else None
        if k == 0:
            emit_prep2b_tail()
            emit_wp_load(k, 0)
            emit_wp_load(k, 1)
        emit_y_group(k, 0)
        emit_y_group(k, 1)
        tok_bc3 = rows_and_bcast(tq3, 3, rs3_t, rm3_t, prev_read=prev_rs3)
        tok_bc3_of[k] = tok_bc3
        setfree(("lnbf",), tu)
        if k + 1 < nsup:
            emit_ln1(k + 1)

    j = nsup - 1
    emit_y_evac(j, 0)
    for d in range(2, ND):
        emit_y_group(j, d)
        emit_y_evac(j, d - 1)
    emit_y_evac(j, ND - 1)
    if debug:
        DBG = {}
        for nm, t_ in (("d_carry", carry), ("d_lnbf", ln_bf), ("d_g", g_sb), ("d_thr", thr),
                       ("d_cpp", c_pp), ("d_wbar", wbar_pp), ("d_clam", clam05),
                       ("d_rs3", rs3_t), ("d_rm3", rm3_t), ("d_wfcq", wfc_q),
                       ("d_ta", t_a), ("d_tb", t_b)):
            sh = [t_.shape[0], t_.shape[1]]
            dt_ = t_.dtype
            Dd = nc.dram_tensor(nm, sh, dt_, kind="ExternalOutput")
            b.wait("gpsimd", tok_yev[(j, ND - 1)])
            DBG[nm] = b.emit("gpsimd", lambda E, Dd=Dd, t_=t_: E.dma_start(Dd[:, :], t_[:, :]), inc="d_y", n=16)
        b.wait("sync", list(DBG.values())[-1])
    b.wait("gpsimd", tok_yev[(j, ND - 1)])
    sj, tj = j // 4, (j % 4) * T
    src = xping[j % 2][:, :].rearrange("p (c t) -> p c t", c=ND)
    dst = Yd[sj, :, tj:tj + T].rearrange("(c p) t -> p c t", p=128)
    tok_ystore[j] = b.emit("gpsimd", lambda E, d=dst, s=src: E.dma_start(d, s), inc="d_y", n=16)
    b.wait("sync", ("d_y", b.semc["d_y"]))

    with nc.Block() as blk:
        @blk.sync
        def _(E):
            for f in b.prog["sync"]:
                f(E)

        @blk.tensor
        def _(E):
            for f in b.prog["tensor"]:
                f(E)

        @blk.vector
        def _(E):
            for f in b.prog["vector"]:
                f(E)

        @blk.scalar
        def _(E):
            for f in b.prog["scalar"]:
                f(E)

        @blk.gpsimd
        def _(E):
            for f in b.prog["gpsimd"]:
                f(E)

    es.close()
    return nc


_NC_CACHE = None


def kernel(L0_attn_x, L0_attn_W_r, L0_attn_W_i, L0_attn_lambd,
           L0_ffn_W_fc, L0_ffn_ln_w, L0_ffn_ln_b, L0_ffn_W_proj):
    global _NC_CACHE
    x = np.ascontiguousarray(np.asarray(L0_attn_x), dtype=np.float32)
    xt = np.ascontiguousarray(x.transpose(0, 2, 1))
    WrT = np.ascontiguousarray(np.asarray(L0_attn_W_r, dtype=np.float32).T)
    WiT = np.ascontiguousarray(np.asarray(L0_attn_W_i, dtype=np.float32).T)
    WfcT = np.ascontiguousarray(np.asarray(L0_ffn_W_fc, dtype=np.float32).T)
    WpT = np.ascontiguousarray(np.asarray(L0_ffn_W_proj, dtype=np.float32).T)
    lam = np.ascontiguousarray(np.asarray(L0_attn_lambd, dtype=np.float32).reshape(ND, 128).T)
    lnw = np.ascontiguousarray(np.asarray(L0_ffn_ln_w, dtype=np.float32).reshape(NH, 128).T)
    lnb = np.ascontiguousarray(np.asarray(L0_ffn_ln_b, dtype=np.float32).reshape(NH, 128).T)

    if _NC_CACHE is None:
        _NC_CACHE = build_nc()
    nc = _NC_CACHE

    in_maps = []
    for c in range(8):
        in_maps.append({
            "x": np.ascontiguousarray(xt[2 * c:2 * c + 2]),
            "wrt": WrT, "wit": WiT, "wfct": WfcT, "wpt": WpT,
            "lam": lam, "lnw": lnw, "lnb": lnb,
        })
    res = run_bass_kernel_spmd(nc, in_maps, core_ids=list(range(8)))
    y = np.concatenate([np.asarray(res.results[c]["y"]) for c in range(8)], axis=0)
    out = np.ascontiguousarray(y.transpose(0, 2, 1), dtype=np.float32)
    return out


# version 29
# speedup vs baseline: 1.0115x; 1.0072x over previous
import numpy as np
import ml_dtypes
from contextlib import ExitStack
import concourse.bass as bass
import concourse.mybir as mybir
from concourse.bass_utils import run_bass_kernel_spmd

F32 = mybir.dt.float32
BF16 = mybir.dt.bfloat16
AF = mybir.ActivationFunctionType
ALU = mybir.AluOpType

D = 1024
ND = 8
H = 4096
NH = 32
T = 512
NSUP = 8
SEQ_T = 2048
NSEQ = 2
EPS = 1e-5
QEPS = 1e-6

ENGS = ("sync", "tensor", "vector", "scalar", "gpsimd")
DPERM = (0, 1, 6, 2, 3, 7, 4, 5)


class B:

    def __init__(self, nc, es):
        self.nc = nc
        self.es = es
        self.prog = {e: [] for e in ENGS}
        self.semh = {}
        self.semc = {}
        self.maxw = {e: {} for e in ENGS}

    def sem(self, name):
        self.semh[name] = self.es.enter_context(self.nc.semaphore(name))
        self.semc[name] = 0

    def emit(self, eng, fn, inc=None, n=1):
        if inc is not None:
            if inc.startswith("d_") and self.semc[inc] > 0:
                self.wait(eng, (inc, self.semc[inc]))
            self.semc[inc] += n
            tok = self.semc[inc]
            h = self.semh[inc]
            self.prog[eng].append(lambda E, fn=fn, h=h, n=n: fn(E).then_inc(h, n))
            return (inc, tok)
        self.prog[eng].append(fn)
        return None

    def wait(self, eng, tok):
        if tok is None:
            return
        name, v = tok
        assert v <= self.semc[name], f"wait on future token {name}={v}"
        if self.maxw[eng].get(name, 0) >= v:
            return
        self.maxw[eng][name] = v
        h = self.semh[name]
        self.prog[eng].append(lambda E, h=h, v=v: E.wait_ge(h, v))


def build_nc(nsup=NSUP, debug=False):
    nc = bass.Bass("TRN2", target_bir_lowering=False, detect_race_conditions=True)
    es = ExitStack()
    b = B(nc, es)
    for s in ("pe", "v", "s", "g", "d_tiny", "d_x0", "d_x1", "d_sc0", "d_sc1", "d_sc2", "d_sc3",
              "d_cw", "d_wr0", "d_wr1", "d_wi0", "d_wi1", "d_wp0", "d_wp1", "d_wp2",
              "d_stg0", "d_stg1", "d_stg2", "d_stg3", "d_y"):
        b.sem(s)

    Xd = nc.dram_tensor("x", [NSEQ, D, SEQ_T], F32, kind="ExternalInput")
    WRT = nc.dram_tensor("wrt", [D, D], F32, kind="ExternalInput")
    WIT = nc.dram_tensor("wit", [D, D], F32, kind="ExternalInput")
    WFCT = nc.dram_tensor("wfct", [D, H], F32, kind="ExternalInput")
    WPT = nc.dram_tensor("wpt", [H, D], F32, kind="ExternalInput")
    LAM = nc.dram_tensor("lam", [128, ND], F32, kind="ExternalInput")
    LNW = nc.dram_tensor("lnw", [128, NH], F32, kind="ExternalInput")
    LNB = nc.dram_tensor("lnb", [128, NH], F32, kind="ExternalInput")
    Yd = nc.dram_tensor("y", [NSEQ, D, SEQ_T], F32, kind="ExternalOutput")
    wr_sc = nc.dram_tensor("wr_sc", [ND, 128, D], BF16, kind="Internal")
    wi_sc = nc.dram_tensor("wi_sc", [ND, 128, D], BF16, kind="Internal")
    wp_sc = nc.dram_tensor("wp_sc", [ND, 128, H], BF16, kind="Internal")
    row_sc = nc.dram_tensor("row_sc", [2, D], F32, kind="Internal")

    sb = lambda name, shape, dt: es.enter_context(nc.sbuf_tensor(name, shape, dt))
    wfc_q = sb("wfc_q", [128, ND * H], BF16)
    wr_sl = sb("wr_sl", [128, 2 * D], BF16)
    wi_sl = sb("wi_sl", [128, 2 * D], BF16)
    wp_sl = sb("wp_sl", [128, 3 * 2048], BF16)
    xb0 = sb("xb0", [128, ND * T], F32)
    xb1 = sb("xb1", [128, ND * T], F32)
    ln_bf = sb("ln_bf", [128, ND * T], BF16)
    xbr = sb("xbr", [128, 2 * T], BF16)
    sqr = sb("sqr", [128, 2 * T], BF16)
    t_r = sb("t_r", [128, T], F32)
    t_a = sb("t_a", [128, 2 * T], F32)
    t_i = sb("t_i", [128, 2 * T], F32)
    t_q = sb("t_q", [128, 2 * T], F32)
    t_b = sb("t_b", [128, 2 * T], F32)
    rs_t = sb("rs_t", [128, T], F32)
    mb_t = sb("mb_t", [128, T], F32)
    rs3_t = sb("rs3_t", [128, T], F32)
    rm3_t = sb("rm3_t", [128, T], F32)
    g_sb = sb("g_sb", [128, NH * T], BF16)
    gsq = sb("gsq", [128, 3 * T], BF16)
    tmp = sb("tmp", [128, T], F32)
    rows = sb("rows", [1, 3 * T], F32)
    lam_pp = sb("lam_pp", [128, ND], F32)
    clam05 = sb("clam05", [128, ND], F32)
    lnw_pp = sb("lnw_pp", [128, NH], F32)
    lnb_pp = sb("lnb_pp", [128, NH], F32)
    lnb_bf = sb("lnb_bf", [128, NH], BF16)
    c_pp = sb("c_pp", [128, ND], F32)
    wbar_pp = sb("wbar_pp", [128, ND], F32)
    carry = sb("carry", [128, ND], F32)
    ones_bf = sb("ones_bf", [128, 128], BF16)
    ones_f = sb("ones_f", [128, 128], F32)
    ones_kf = sb("ones_kf", [1, 128], F32)
    accf = sb("accf", [128, 32], F32)
    accp = sb("accp", [128, 32], F32)
    thr = sb("thr", [128, 6], F32)
    cst = sb("cst", [128, 8], F32)
    qn_buf = sb("qn_buf", [128, 4 * 1024], BF16)
    lnw_bf = sb("lnw_bf", [128, NH], BF16)
    lnmc_pp = sb("lnmc_pp", [128, ND], F32)
    lnbw_f = sb("lnbw_f", [128, 2 * NH], F32)
    lnbw_bf = sb("lnbw_bf", [128, 2 * NH], BF16)

    ps = lambda name: es.enter_context(nc.psum_tensor(name, [128, T], F32))
    pool = [ps(f"pool{j}") for j in range(4)]
    pY = [ps("pY0"), ps("pY1")]
    pS = ps("pS")
    pQ = ps("pQ")

    stg0 = xb1[:, 0:1024]
    stg1 = xb1[:, 1024:2048]
    xstg = [xb1[:, j * 1024:(j + 1) * 1024] for j in range(4)]
    s1 = rows[0:1, 0:T]
    s2 = rows[0:1, T:2 * T]
    s3 = rows[0:1, 2 * T:3 * T]
    xping = [xb0, xb1]

    free = {}

    def usewait(eng, key):
        if key in free:
            t = free[key]
            for tt in (t if isinstance(t, list) else [t]):
                b.wait(eng, tt)

    def setfree(key, tok):
        free[key] = tok

    xch = lambda buf, c: buf[:, c * T:(c + 1) * T]
    lnch = lambda c: ln_bf[:, c * T:(c + 1) * T]
    gch = lambda h: g_sb[:, h * T:(h + 1) * T]

    tok_x = {}
    b.emit("sync", lambda E: E.dma_start(lam_pp[:, :], LAM[:, :]), inc="d_tiny", n=16)
    b.emit("sync", lambda E: E.dma_start(lnw_pp[:, :], LNW[:, :]), inc="d_tiny", n=16)
    t_tiny = b.emit("sync", lambda E: E.dma_start(lnb_pp[:, :], LNB[:, :]), inc="d_tiny", n=16)
    for k in (0,):
        seq, t0 = k // 4, (k % 4) * T
        dst = xping[k % 2][:, :].rearrange("p (c t) -> p c t", c=ND)
        src = Xd[seq, :, t0:t0 + T].rearrange("(c p) t -> p c t", p=128)
        tok_x[k] = b.emit("sync", lambda E, d=dst, s=src: E.dma_start(d, s), inc=f"d_x{k % 2}", n=16)

    b.emit("vector", lambda E: E.memset(cst[:, 0:1], 0.5))
    b.emit("vector", lambda E: E.memset(cst[:, 1:2], -0.5))
    b.emit("vector", lambda E: E.memset(cst[:, 2:3], 1.0 / 1024.0))
    b.emit("vector", lambda E: E.memset(cst[:, 3:4], 1.0 / 4096.0))
    b.emit("vector", lambda E: E.memset(cst[:, 4:5], EPS))
    b.emit("vector", lambda E: E.memset(cst[:, 5:6], 4.0))
    b.emit("vector", lambda E: E.memset(cst[:, 6:7], -8.0))
    b.emit("vector", lambda E: E.memset(ones_bf[:, :], 1.0))
    b.emit("vector", lambda E: E.memset(ones_f[:, :], 1.0))
    tok_ones = b.emit("vector", lambda E: E.memset(ones_kf[:, :], 1.0), inc="v")
    b.wait("vector", t_tiny)
    b.emit("vector", lambda E: E.tensor_copy(lnb_bf[:, :], lnb_pp[:, :]))
    tok_lnbbf = b.emit("vector", lambda E: E.tensor_copy(lnw_bf[:, :], lnw_pp[:, :]), inc="v")
    lnbw_v = lnbw_f.rearrange("p (c two) -> p c two", two=2)
    b.emit("vector", lambda E: E.tensor_copy(lnbw_v[:, :, 0], lnb_pp[:, :]))
    tok_lnbw0 = b.emit("vector", lambda E: E.tensor_copy(lnbw_v[:, :, 1], lnw_pp[:, :]), inc="v")
    b.wait("vector", tok_lnbw0)
    tok_lnbw = b.emit("vector", lambda E: E.tensor_copy(lnbw_bf[:, :], lnbw_f[:, :]), inc="v")
    b.wait("scalar", t_tiny)
    b.wait("scalar", tok_ones)
    tA = b.emit("scalar", lambda E: E.activation(clam05[:, :], lam_pp[:, :], AF.Exp, scale=-1.0), inc="s")
    b.wait("scalar", tA)
    tA = b.emit("scalar", lambda E: E.activation(clam05[:, :], clam05[:, :], AF.Ln, bias=1.0), inc="s")
    b.wait("scalar", tA)
    tA = b.emit("scalar", lambda E: E.activation(clam05[:, :], clam05[:, :], AF.Copy, scale=cst[:, 6:7]), inc="s")
    b.wait("scalar", tA)
    tok_clam = b.emit("scalar", lambda E: E.activation(lnmc_pp[:, :], clam05[:, :], AF.Ln, scale=-1.0), inc="s")

    qn_s = lambda i: qn_buf[:, (i % 4) * 1024:((i % 4) + 1) * 1024]
    sc_last = {}

    def emit_scatter(slot, dst, src_ap):
        sem = f"d_sc{slot % 4}"
        qeng = "sync" if slot % 2 == 0 else "gpsimd"
        tw = b.emit(qeng, lambda E, d=dst, s=src_ap: E.dma_start(d, s), inc=sem, n=16)
        setfree(("qn4", slot % 4), tw)
        sc_last[sem] = tw
        return tw

    for widx, (WSRC, WSC) in enumerate(((WRT, wr_sc), (WIT, wi_sc))):
        for kd in range(ND):
            slot = (widx * ND + kd) % 4
            st = stg0 if kd % 2 == 0 else stg1
            qeng = "sync" if kd % 2 == 0 else "gpsimd"
            usewait(qeng, ("stg", kd % 2))
            tl = b.emit(qeng, lambda E, st=st, W=WSRC, kd=kd:
                        E.dma_start(st, W[128 * kd:128 * (kd + 1), :]), inc=f"d_stg{kd % 2}", n=16)
            b.wait("vector", tl)
            usewait("vector", ("qn4", slot))
            tc = b.emit("vector", lambda E, sl=qn_s(slot), st=st: E.tensor_copy(sl, st), inc="v")
            setfree(("stg", kd % 2), tc)
            qw = "sync" if slot % 2 == 0 else "gpsimd"
            b.wait(qw, tc)
            dst = WSC[:, :, kd * 128:(kd + 1) * 128].transpose([1, 0, 2])
            src2 = qn_s(slot).rearrange("p (ke j) -> p ke j", ke=ND)
            emit_scatter(slot, dst, src2)
    tok_wrsc = list(sc_last.values())
    tok_wisc = []

    tstg = [t_a[:, :], t_q[:, :], t_b[:, :], t_i[:, :]]
    tkeymap = ["t_a", "t_q", "t_b", "t_i"]
    gfc_state = {"last": None}
    tok_wfc_pc = {}
    p2b_defer = {}

    gfc_toks = {}

    def emit_gfc_loads(qi):
        for j in range(4):
            i = 4 * qi + j
            pc, kd = i // 8, i % 8
            sl = i % 4
            st = xstg[sl]
            qeng = "sync" if sl % 2 == 0 else "gpsimd"
            usewait(qeng, ("stg", sl))
            gfc_toks[i] = b.emit(qeng, lambda E, st=st, kd=kd, pc=pc:
                                 E.dma_start(st, WFCT[128 * kd:128 * (kd + 1), 1024 * pc:1024 * (pc + 1)]),
                                 inc=f"d_stg{sl}", n=16)

    def emit_gfc_abs(qi):
        for j in range(4):
            i = 4 * qi + j
            sl = i % 4
            st = xstg[sl]
            b.wait("scalar", gfc_toks[i])
            red = b.emit("scalar", lambda E, st=st, i=i:
                         E.activation(st, st, AF.Abs, accum_out=accf[:, i:i + 1]), inc="s")
            setfree(("stg", sl), red)
            gfc_state["last"] = red

    def emit_thr(acc, pool_bank, col):
        usewait("tensor", ("pool", pool_bank))
        tg = b.emit("tensor", lambda E: E.matmul(pool[pool_bank][:, 0:32], ones_f[:, :], acc[:, 0:32],
                                                 start=True, stop=True), inc="pe")
        b.wait("vector", tg)
        tA = b.emit("vector", lambda E: E.tensor_reduce(thr[:, col:col + 1], pool[pool_bank][:, 0:32],
                                                        mybir.AxisListType.X, ALU.add), inc="v")
        setfree(("pool", pool_bank), tA)
        b.wait("vector", tA)
        tA = b.emit("vector", lambda E: E.tensor_scalar(thr[:, col:col + 1], thr[:, col:col + 1],
                                                        1.0 / (H * D), QEPS, ALU.mult, ALU.add), inc="v")
        b.wait("vector", tA)
        b.emit("vector", lambda E: E.tensor_scalar_mul(thr[:, col + 1:col + 2], thr[:, col:col + 1], 0.5))
        return b.emit("vector", lambda E: E.tensor_scalar_mul(thr[:, col + 2:col + 3], thr[:, col:col + 1], -0.5), inc="v")

    def emit_ln2_tile_one(c, xb, tok_bc2, tok_i7, tok_gbl):
        eng, scr, inc, key = (("vector", tmp, "v", ("tmp",)) if c < 6 else
                              ("gpsimd", t_r, "g", ("t_r",)))
        b.wait(eng, tok_bc2)
        b.wait(eng, tok_i7)
        b.wait(eng, tok_gbl)
        usewait(eng, ("lnbf",))
        usewait(eng, key)
        tA = b.emit(eng, lambda E, c=c, xb=xb, scr=scr: E.tensor_tensor(scr[:, :], xch(xb, c), rs_t[:, :], ALU.mult), inc=inc)
        b.wait(eng, tA)
        tln2 = b.emit(eng, lambda E, c=c, scr=scr: E.tensor_tensor(lnch(c), scr[:, :], mb_t[:, :], ALU.add), inc=inc)
        setfree(key, tln2)
        return tln2

    def emit_prep2a_mid(xb, tok_bc2, tok_i7, tok_gbl):
        nonlocal tok_wfc, tok_thrp
        ln2_toks = {}
        b.wait("tensor", gfc_state["last"])
        b.wait("tensor", tok_ones)
        tok_thrf = emit_thr(accf, 0, 0)
        red_p = None
        for i in range(32):
            pc, kd = i // 8, i % 8
            sl = i % 4
            st = xstg[sl]
            qeng = "sync" if sl % 2 == 0 else "gpsimd"
            usewait(qeng, ("stg", sl))
            tl = b.emit(qeng, lambda E, st=st, kd=kd, pc=pc:
                        E.dma_start(st, WFCT[128 * kd:128 * (kd + 1), 1024 * pc:1024 * (pc + 1)]),
                        inc=f"d_stg{sl}", n=16)
            st2 = tstg[sl]
            qeng2 = "sync" if sl % 2 == 0 else "gpsimd"
            if i < 4:
                usewait(qeng2, (tkeymap[sl], 0))
                usewait(qeng2, (tkeymap[sl], 1))
            else:
                usewait(qeng2, ("tstg", sl))
            tl2 = b.emit(qeng2, lambda E, st2=st2, i=i:
                         E.dma_start(st2, WPT[128 * i:128 * (i + 1), :]), inc=f"d_stg{sl}", n=16)
            b.wait("vector", tl)
            b.wait("vector", tok_thrf)
            usewait("vector", ("qn4", sl))
            tA2 = b.emit("vector", lambda E, st=st, sl=sl:
                         E.tensor_single_scalar(qn_s(sl), st, thr[:, 2:3], ALU.is_le), inc="v")
            b.wait("vector", tA2)
            dstq = wfc_q[:, kd * H + pc * 1024: kd * H + (pc + 1) * 1024]
            tq = b.emit("vector", lambda E, st=st, d=dstq, sl=sl:
                        E.scalar_tensor_tensor(d, st, thr[:, 1:2], qn_s(sl),
                                               ALU.is_ge, ALU.subtract), inc="v")
            setfree(("qn4", sl), tq)
            setfree(("stg", sl), tq)
            if kd == 7:
                tok_wfc_pc[pc] = tq
            b.wait("scalar", tl2)
            red_p = b.emit("scalar", lambda E, st2=st2, i=i:
                           E.activation(st2, st2, AF.Abs, accum_out=accp[:, i:i + 1]), inc="s")
            setfree(("tstg", sl), red_p)
            if i % 2 == 0 and i // 2 < ND:
                ln2_toks[i // 2] = emit_ln2_tile_one(i // 2, xb, tok_bc2, tok_i7, tok_gbl)
        tok_wfc = tok_wfc_pc[3]
        b.wait("tensor", red_p)
        tok_thrp = emit_thr(accp, 1, 3)
        for sl in range(4):
            usewait("sync", ("stg", sl))
        dst1 = xping[1][:, :].rearrange("p (c t) -> p c t", c=ND)
        src1 = Xd[0, :, T:2 * T].rearrange("(c p) t -> p c t", p=128)
        tok_x[1] = b.emit("sync", lambda E, d=dst1, s=src1: E.dma_start(d, s), inc="d_x1", n=16)
        return [ln2_toks[c] for c in range(ND)]

    p2b_loads = {}

    def emit_prep2b_load(i):
        sl = i % 4
        st = tstg[sl]
        qeng = "sync" if sl % 2 == 0 else "gpsimd"
        usewait(qeng, ("tstg", sl))
        p2b_loads[i] = b.emit(qeng, lambda E, st=st, i=i:
                              E.dma_start(st, WPT[128 * i:128 * (i + 1), :]), inc=f"d_stg{sl}", n=16)

    def emit_prep2b_piece(i):
        nonlocal tok_cw
        sl = i % 4
        st = tstg[sl]
        tl = p2b_loads.pop(i)
        b.wait("vector", tl)
        b.wait("vector", tok_thrp)
        usewait("vector", ("qn4", sl))
        tA4 = b.emit("vector", lambda E, st=st, sl=sl:
                     E.tensor_single_scalar(qn_s(sl), st, thr[:, 5:6], ALU.is_le), inc="v")
        b.wait("vector", tA4)
        tq1 = b.emit("vector", lambda E, st=st, sl=sl:
                     E.scalar_tensor_tensor(qn_s(sl), st, thr[:, 4:5], qn_s(sl),
                                            ALU.is_ge, ALU.subtract), inc="v")
        setfree(("tstg", sl), tq1)
        b.wait("tensor", tq1)
        b.wait("tensor", tok_lnbw)
        qh = qn_s(sl)
        b.emit("tensor", lambda E, i=i, qh=qh: E.matmul(pY[0][0:2, :], lnbw_bf[:, 2 * i:2 * i + 2],
                                                        qh[:, 0:512], start=(i == 0), stop=(i == 31)))
        tw2 = b.emit("tensor", lambda E, i=i, qh=qh: E.matmul(pY[1][0:2, :], lnbw_bf[:, 2 * i:2 * i + 2],
                                                              qh[:, 512:1024], start=(i == 0), stop=(i == 31)),
                     inc="pe")
        if i == 31:
            tok_cw = tw2
        p2b_defer[i] = (qh, tw2)

    def emit_prep2b_fold(i):
        qh, tw2 = p2b_defer.pop(i)
        b.wait("vector", tw2)
        tq2 = b.emit("vector", lambda E, i=i, qh=qh:
                     E.tensor_scalar_mul(qh, qh, lnw_pp[:, i:i + 1]), inc="v")
        qw = "sync" if i % 2 == 0 else "gpsimd"
        b.wait(qw, tq2)
        dst = wp_sc[:, :, i * 128:(i + 1) * 128].transpose([1, 0, 2])
        src2 = qh.rearrange("p (kd j) -> p kd j", kd=ND)
        emit_scatter(i, dst, src2)

    def emit_prep2b_tail():
        nonlocal tok_cwe, tok_cpp, tok_wpsc
        emit_prep2b_fold(NH - 2)
        emit_prep2b_fold(NH - 1)
        tok_wpsc = [sc_last[s] for s in ("d_sc0", "d_sc1", "d_sc2", "d_sc3")]
        for sl in range(4):
            tok = free[("tstg", sl)]
            setfree((tkeymap[sl], 0), tok)
            setfree((tkeymap[sl], 1), tok)
        b.wait("vector", tok_cw)
        usewait("vector", ("tmp",))
        b.emit("vector", lambda E: E.tensor_copy(tmp[0:2, :], pY[0][0:2, :]))
        tok_cwe = b.emit("vector", lambda E: E.tensor_copy(tmp[32:34, :], pY[1][0:2, :]), inc="v")
        setfree(("pY", 0), tok_cwe)
        setfree(("pY", 1), tok_cwe)
        b.wait("sync", tok_cwe)
        b.emit("sync", lambda E: E.dma_start(row_sc[0:1, 0:512], tmp[0:1, :]), inc="d_cw", n=16)
        b.emit("sync", lambda E: E.dma_start(row_sc[0:1, 512:1024], tmp[32:33, :]), inc="d_cw", n=16)
        b.emit("sync", lambda E: E.dma_start(row_sc[1:2, 0:512], tmp[1:2, :]), inc="d_cw", n=16)
        trw = b.emit("sync", lambda E: E.dma_start(row_sc[1:2, 512:1024], tmp[33:34, :]), inc="d_cw", n=16)
        b.wait("sync", trw)
        b.emit("sync", lambda E: E.dma_start(c_pp[:, :], row_sc[0, :].rearrange("(c p) -> p c", p=128),
                                             allow_slow_non_contiguous=True), inc="d_cw", n=16)
        tok_cpp = b.emit("sync", lambda E: E.dma_start(wbar_pp[:, :], row_sc[1, :].rearrange("(c p) -> p c", p=128),
                                                       allow_slow_non_contiguous=True), inc="d_cw", n=16)
        setfree(("tmp",), tok_cpp)

    tok_wfc = tok_thrp = tok_cw = tok_cwe = tok_cpp = None
    tok_wpsc = []

    def ln_stats_mm(xbuf):
        for c in range(ND):
            sl = c % 2
            usewait("vector", ("xbr", sl))
            tA = b.emit("vector", lambda E, c=c, sl=sl, xb=xbuf: E.tensor_copy(xbr[:, sl * T:(sl + 1) * T], xch(xb, c)), inc="v")
            b.wait("vector", tA)
            tsq = b.emit("vector", lambda E, sl=sl:
                         E.tensor_tensor(sqr[:, sl * T:(sl + 1) * T], xbr[:, sl * T:(sl + 1) * T],
                                         xbr[:, sl * T:(sl + 1) * T], ALU.mult), inc="v")
            b.wait("tensor", tsq)
            if c == 0:
                usewait("tensor", ("pS",))
                usewait("tensor", ("pQ",))
            b.emit("tensor", lambda E, c=c, sl=sl: E.matmul(pS[:, :], ones_bf[:, :], xbr[:, sl * T:(sl + 1) * T],
                                                            start=(c == 0), stop=(c == ND - 1)))
            tst = b.emit("tensor", lambda E, c=c, sl=sl: E.matmul(pQ[:, :], ones_bf[:, :], sqr[:, sl * T:(sl + 1) * T],
                                                                  start=(c == 0), stop=(c == ND - 1)), inc="pe")
            setfree(("xbr", sl), tst)
        return tst

    def rows_and_bcast(tstats, divcol, RS, MB, prev_read=None, prev_read_g=None):
        b.wait("scalar", tstats)
        usewait("scalar", ("rows",))
        b.emit("scalar", lambda E: E.activation(s1, pS[0:1, :], AF.Copy, scale=cst[0:1, divcol:divcol + 1]))
        trow = b.emit("scalar", lambda E: E.activation(s2, pQ[0:1, :], AF.Copy, scale=cst[0:1, divcol:divcol + 1]), inc="s")
        b.wait("vector", trow)
        tA = b.emit("vector", lambda E: E.tensor_tensor(s3, s1, s1, ALU.mult), inc="v")
        b.wait("vector", tA)
        tv1 = b.emit("vector", lambda E: E.scalar_tensor_tensor(s3, s3, -1.0, s2, ALU.mult, ALU.add), inc="v")
        b.wait("scalar", tv1)
        tA = b.emit("scalar", lambda E: E.activation(s3, s3, AF.Ln, bias=cst[0:1, 4:5]), inc="s")
        b.wait("scalar", tA)
        ts2 = b.emit("scalar", lambda E: E.activation(s2, s3, AF.Exp, scale=cst[0:1, 1:2]), inc="s")
        b.wait("vector", ts2)
        trow2 = b.emit("vector", lambda E: E.scalar_tensor_tensor(s1, s1, -1.0, s2, ALU.mult, ALU.mult), inc="v")
        b.wait("tensor", trow2)
        tb1 = b.emit("tensor", lambda E: E.matmul(pS[:, :], ones_kf[:, :], s2, start=True, stop=True), inc="pe")
        tb2 = b.emit("tensor", lambda E: E.matmul(pQ[:, :], ones_kf[:, :], s1, start=True, stop=True), inc="pe")
        setfree(("rows",), tb2)
        b.wait("vector", tb1)
        if prev_read is not None:
            b.wait("vector", prev_read)
        if prev_read_g is not None:
            b.wait("vector", prev_read_g)
        b.emit("vector", lambda E, R=RS: E.tensor_copy(R[:, :], pS[:, :]), inc="v")
        b.wait("vector", tb2)
        tbc = b.emit("vector", lambda E, M=MB: E.tensor_copy(M[:, :], pQ[:, :]), inc="v")
        setfree(("pS",), tbc)
        setfree(("pQ",), tbc)
        return tbc

    def emit_y_group(j, d):
        bank = d % 2
        usewait("tensor", ("pY", bank))
        tok = None
        for half in range(2):
            hh = d * 2 + half
            b.wait("tensor", tok_wp[(j, hh)])
            for kk in range(16):
                kh = half * 16 + kk
                sl = hh % 3
                last = (half == 1 and kk == 15)
                fn = lambda E, kh=kh, sl=sl, bank=bank, first=(kh == 0), last=last: E.matmul(
                    pY[bank][:, :], wp_sl[:, sl * 2048 + (kh % 16) * 128: sl * 2048 + (kh % 16) * 128 + 128],
                    gch(kh), start=first, stop=last)
                if kk == 15:
                    tok = b.emit("tensor", fn, inc="pe")
                    setfree(("wpsl", sl), tok)
                else:
                    b.emit("tensor", fn)
        tok_ymm[(j, d)] = tok
        for hh2 in (2 * d + 2, 2 * d + 3):
            if hh2 < 2 * ND and (j, hh2) not in tok_wp:
                emit_wp_load(j, hh2)

    def emit_y_evac(j, d):
        bank = d % 2
        xb = xping[j % 2]
        b.wait("vector", tok_ymm[(j, d)])
        b.wait("vector", tok_bc3_of[j])
        b.wait("vector", tok_cpp)
        usewait("vector", ("tmp",))
        tA = b.emit("vector", lambda E, bank=bank: E.tensor_tensor(tmp[:, :], pY[bank][:, :], rs3_t[:, :], ALU.mult), inc="v")
        b.wait("vector", tA)
        tfree = b.emit("vector", lambda E, d=d: E.scalar_tensor_tensor(
            tmp[:, :], rm3_t[:, :], wbar_pp[:, d:d + 1], tmp[:, :], ALU.mult, ALU.add), inc="v")
        setfree(("pY", bank), tfree)
        b.wait("vector", tfree)
        tok = b.emit("vector", lambda E, d=d, xb=xb: E.scalar_tensor_tensor(
            xch(xb, d), tmp[:, :], c_pp[:, d:d + 1], xch(xb, d), ALU.add, ALU.add), inc="v")
        setfree(("tmp",), tok)
        tok_yev[(j, d)] = tok

    def emit_wri_load(k, e):
        sl = (k * ND + e) % 2
        usewait("sync", ("wrsl", sl))
        for t_ in tok_wrsc:
            b.wait("sync", t_)
        tok_wr[(k, e, 0)] = b.emit("sync", lambda E, e=e, sl=sl:
                                   E.dma_start(wr_sl[:, sl * D:(sl + 1) * D], wr_sc[e, :, :]), inc=f"d_wr{sl}", n=16)
        usewait("sync", ("wisl", sl))
        for t_ in tok_wisc:
            b.wait("sync", t_)
        tok_wr[(k, e, 1)] = b.emit("sync", lambda E, e=e, sl=sl:
                                   E.dma_start(wi_sl[:, sl * D:(sl + 1) * D], wi_sc[e, :, :]), inc=f"d_wi{sl}", n=16)

    def emit_wp_load(j, hh):
        d, half = hh // 2, hh % 2
        sl = hh % 3
        usewait("sync", ("wpsl", sl))
        for t_ in tok_wpsc:
            b.wait("sync", t_)
        tok_wp[(j, hh)] = b.emit("sync", lambda E, d=d, half=half, sl=sl:
                                 E.dma_start(wp_sl[:, sl * 2048:(sl + 1) * 2048],
                                             wp_sc[d, :, half * 2048:(half + 1) * 2048]), inc=f"d_wp{sl}", n=16)

    tok_wr = {}
    tok_wp = {}
    tok_ri = {}
    tok_ymm = {}
    tok_yev = {}
    tok_ystore = {}
    tok_gelu = {}

    tok_bc3_of = {}
    tok_ln1_of = {}
    tok_ln2_of = {}

    tok_x1_of = {}

    def emit_ln2_stat_chunk(k, xbuf, c):
        sl = c % 2
        eng, inc = ("gpsimd", "g") if c < 7 else ("vector", "v")
        b.wait(eng, tok_x1_of[c])
        usewait(eng, ("xbr", sl))
        tA = b.emit(eng, lambda E, c=c, sl=sl, xb=xbuf: E.tensor_copy(xbr[:, sl * T:(sl + 1) * T], xch(xb, c)), inc=inc)
        b.wait(eng, tA)
        tsq = b.emit(eng, lambda E, sl=sl:
                     E.tensor_tensor(sqr[:, sl * T:(sl + 1) * T], xbr[:, sl * T:(sl + 1) * T],
                                     xbr[:, sl * T:(sl + 1) * T], ALU.mult), inc=inc)
        b.wait("tensor", tsq)
        if c == 0:
            usewait("tensor", ("pS",))
            usewait("tensor", ("pQ",))
        b.emit("tensor", lambda E, c=c, sl=sl: E.matmul(pS[:, :], ones_bf[:, :], xbr[:, sl * T:(sl + 1) * T],
                                                        start=(c == 0), stop=(c == ND - 1)))
        tst = b.emit("tensor", lambda E, c=c, sl=sl: E.matmul(pQ[:, :], ones_bf[:, :], sqr[:, sl * T:(sl + 1) * T],
                                                              start=(c == 0), stop=(c == ND - 1)), inc="pe")
        setfree(("xbr", sl), tst)
        return tst

    def emit_ln1(k):
        xb = xping[k % 2]
        b.wait("vector", tok_x[k])
        tstat1 = ln_stats_mm(xb)
        pr = tok_ln2_of.get(k - 1)
        tok_bc1 = rows_and_bcast(tstat1, 2, rs_t, mb_t, prev_read=pr[5] if pr else None,
                                 prev_read_g=pr[7] if pr else None)
        lt = []
        for c in range(ND):
            eng, scr, inc, key = (("vector", tmp, "v", ("tmp",)) if c < 6 else
                                  ("gpsimd", t_r, "g", ("t_r",)))
            b.wait(eng, tok_bc1)
            usewait(eng, ("lnbf",))
            usewait(eng, key)
            tA = b.emit(eng, lambda E, c=c, xb=xb, scr=scr: E.tensor_tensor(scr[:, :], xch(xb, c), rs_t[:, :], ALU.mult), inc=inc)
            b.wait(eng, tA)
            tln = b.emit(eng, lambda E, c=c, scr=scr: E.tensor_tensor(lnch(c), scr[:, :], mb_t[:, :], ALU.add), inc=inc)
            setfree(key, tln)
            lt.append(tln)
        tok_ln1_of[k] = lt
        free.pop(("lnbf",), None)

    emit_ln1(0)

    for k in range(nsup):
        xb = xping[k % 2]
        seq, t0 = k // 4, (k % 4) * T
        ln1lt = tok_ln1_of[k]
        tok_ln1, tok_ln1g = ln1lt[5], ln1lt[7]

        if k == 0:
            for e in (0, 1):
                emit_wri_load(k, e)
            emit_gfc_loads(0)

        for e in range(ND):
            q = e % 2
            pR = pool[2 * q]
            pI = pool[2 * q + 1]
            b.wait("tensor", tok_wr[(k, e, 0)])
            usewait("tensor", ("pool", 2 * q))
            sl = (k * ND + e) % 2
            for ii, d in enumerate(DPERM):
                b.wait("tensor", ln1lt[d])
                fn = lambda E, d=d, sl=sl, pR=pR, ii=ii: E.matmul(
                    pR[:, :], wr_sl[:, sl * D + d * 128: sl * D + d * 128 + 128], lnch(d),
                    start=(ii == 0), stop=(ii == ND - 1))
                if ii == ND - 1:
                    tok_r = b.emit("tensor", fn, inc="pe")
                    setfree(("wrsl", sl), tok_r)
                else:
                    b.emit("tensor", fn)
            b.wait("tensor", tok_wr[(k, e, 1)])
            usewait("tensor", ("pool", 2 * q + 1))
            for ii, d in enumerate(DPERM):
                fn = lambda E, d=d, sl=sl, pI=pI, ii=ii: E.matmul(
                    pI[:, :], wi_sl[:, sl * D + d * 128: sl * D + d * 128 + 128], lnch(d),
                    start=(ii == 0), stop=(ii == ND - 1))
                if ii == ND - 1:
                    tok_i = b.emit("tensor", fn, inc="pe")
                    setfree(("wisl", sl), tok_i)
                else:
                    b.emit("tensor", fn)
            tok_ri[(k, e)] = (tok_r, tok_i)
            if e + 2 < ND:
                emit_wri_load(k, e + 2)
            if k >= 1 and 1 <= e <= 7:
                emit_y_evac(k - 1, e - 1)
            if k >= 1 and 1 <= e <= 6:
                emit_y_group(k - 1, e + 1)
            if k == 0:
                emit_gfc_abs(e)
                if e + 1 < ND:
                    emit_gfc_loads(e + 1)

            b.wait("scalar", tok_r)
            b.wait("scalar", tok_clam)
            usewait("scalar", ("t_a", q))
            usewait("scalar", ("t_r",))
            tA = b.emit("scalar", lambda E, pR=pR: E.activation(t_r[:, :], pR[:, :], AF.Exp, scale=-1.0), inc="s")
            setfree(("pool", 2 * q), tA)
            b.wait("scalar", tA)
            tA = b.emit("scalar", lambda E: E.activation(t_r[:, :], t_r[:, :], AF.Ln, bias=1.0), inc="s")
            b.wait("scalar", tA)
            usewait("scalar", ("t_a", q))
            tA = b.emit("scalar", lambda E, q=q, e=e: E.activation(
                t_a[:, q * T:(q + 1) * T], t_r[:, :], AF.Exp, scale=-1.0,
                bias=lnmc_pp[:, e:e + 1]), inc="s")
            setfree(("t_r",), tA)
            b.wait("scalar", tA)
            tfR = b.emit("scalar", lambda E, q=q: E.activation(
                t_a[:, q * T:(q + 1) * T], t_a[:, q * T:(q + 1) * T], AF.Exp, scale=-1.0), inc="s")
            b.wait("scalar", tok_i)
            usewait("scalar", ("t_i", q))
            tA = b.emit("scalar", lambda E, pI=pI, q=q: E.activation(
                t_i[:, q * T:(q + 1) * T], pI[:, :], AF.Exp, scale=-1.0), inc="s")
            setfree(("pool", 2 * q + 1), tA)
            b.wait("scalar", tA)
            tvi = b.emit("scalar", lambda E, q=q: E.activation(
                t_i[:, q * T:(q + 1) * T], t_i[:, q * T:(q + 1) * T], AF.Ln, bias=1.0), inc="s")
            b.wait("vector", tfR)
            usewait("vector", ("t_q", q))
            tv1 = b.emit("vector", lambda E, q=q: E.scalar_tensor_tensor(
                t_q[:, q * T:(q + 1) * T], t_a[:, q * T:(q + 1) * T], -1.0,
                t_a[:, q * T:(q + 1) * T], ALU.mult, ALU.mult), inc="v")
            b.wait("scalar", tv1)
            tA = b.emit("scalar", lambda E, q=q: E.activation(
                t_q[:, q * T:(q + 1) * T], t_q[:, q * T:(q + 1) * T], AF.Ln, bias=1.0), inc="s")
            b.wait("vector", tA)
            b.wait("vector", tvi)
            tvm = b.emit("vector", lambda E, q=q: E.scalar_tensor_tensor(
                t_q[:, q * T:(q + 1) * T], t_q[:, q * T:(q + 1) * T], 0.5,
                t_i[:, q * T:(q + 1) * T], ALU.mult, ALU.subtract), inc="v")
            setfree(("t_i", q), tvm)
            b.wait("scalar", tvm)
            tst = b.emit("scalar", lambda E, q=q: E.activation(
                t_q[:, q * T:(q + 1) * T], t_q[:, q * T:(q + 1) * T], AF.Exp), inc="s")
            b.wait("vector", tst)
            b.wait("vector", ln1lt[e])
            usewait("vector", ("t_b", q))
            tgb = b.emit("vector", lambda E, q=q, e=e: E.tensor_tensor(
                t_b[:, q * T:(q + 1) * T], t_q[:, q * T:(q + 1) * T], lnch(e), ALU.mult), inc="v")
            setfree(("t_q", q), tgb)
            b.wait("vector", tgb)
            init = 0.0 if (k % 4 == 0) else carry[:, e:e + 1]
            tA = b.emit("vector", lambda E, q=q, init=init: E.tensor_tensor_scan(
                t_b[:, q * T:(q + 1) * T], t_a[:, q * T:(q + 1) * T], t_b[:, q * T:(q + 1) * T],
                init, ALU.mult, ALU.add), inc="v")
            b.wait("vector", tA)
            b.emit("vector", lambda E, q=q, e=e: E.tensor_copy(carry[:, e:e + 1], t_b[:, q * T + T - 1: q * T + T]))
            tx1 = b.emit("vector", lambda E, q=q, e=e, xb=xb: E.tensor_tensor(
                xch(xb, e), xch(xb, e), t_b[:, q * T:(q + 1) * T], ALU.add), inc="v")
            setfree(("t_a", q), tx1)
            setfree(("t_b", q), tx1)
            tok_gb_last = tgb
            tok_x1_of[e] = tx1
            if e >= 1:
                emit_ln2_stat_chunk(k, xb, e - 1)

        tstat2 = emit_ln2_stat_chunk(k, xb, ND - 1)
        tok_bc2 = rows_and_bcast(tstat2, 2, rs_t, mb_t, prev_read=tok_ln1, prev_read_g=tok_ln1g)
        if k == 0:
            tok_ln2 = emit_prep2a_mid(xb, tok_bc2, tok_ri[(k, ND - 1)][1], tok_x1_of[ND - 1])
        else:
            lt = [emit_ln2_tile_one(c, xb, tok_bc2, tok_ri[(k, ND - 1)][1], tok_x1_of[ND - 1])
                  for c in range(ND)]
            tok_ln2 = lt
        tok_ln2_of[k] = tok_ln2
        if k >= 1:
            emit_y_evac(k - 1, 7)
            j = k - 1
            b.wait("sync", tok_yev[(j, ND - 1)])
            sj, tj = j // 4, (j % 4) * T
            src = xping[j % 2][:, :].rearrange("p (c t) -> p c t", c=ND)
            dst = Yd[sj, :, tj:tj + T].rearrange("(c p) t -> p c t", p=128)
            tok_ystore[j] = b.emit("sync", lambda E, d=dst, s=src: E.dma_start(d, s), inc="d_y", n=16)
            if k + 1 < nsup:
                kk = k + 1
                b.wait("sync", tok_ystore.get(kk - 2, None))
                sq2, t02 = kk // 4, (kk % 4) * T
                dst2 = xping[kk % 2][:, :].rearrange("p (c t) -> p c t", c=ND)
                src2 = Xd[sq2, :, t02:t02 + T].rearrange("(c p) t -> p c t", p=128)
                tok_x[kk] = b.emit("sync", lambda E, d=dst2, s=src2: E.dma_start(d, s), inc=f"d_x{kk % 2}", n=16)

        if k > 0:
            b.wait("tensor", tok_wfc)
        else:
            for i3 in range(3):
                emit_prep2b_load(i3)
        gs_toks = {}

        def emit_ln3_stat(h):
            tge2, tgs2 = gs_toks[h]
            b.wait("tensor", tge2)
            b.wait("tensor", tgs2)
            if h == 0:
                usewait("tensor", ("pS",))
                usewait("tensor", ("pQ",))
            b.emit("tensor", lambda E, h=h: E.matmul(pS[:, :], ones_bf[:, :], gch(h),
                                                     start=(h == 0), stop=(h == NH - 1)))
            tq = b.emit("tensor", lambda E, h=h: E.matmul(pQ[:, :], ones_bf[:, :],
                                                          gsq[:, (h % 3) * T:(h % 3 + 1) * T],
                                                          start=(h == 0), stop=(h == NH - 1)), inc="pe")
            setfree(("gsq", h % 3), tq)
            return tq

        for h in range(NH):
            bank = h % 4
            if k == 0:
                b.wait("tensor", tok_wfc_pc[h // 8])
            usewait("tensor", ("pool", bank))
            for ii, d in enumerate(DPERM):
                b.wait("tensor", tok_ln2[d])
                fn = lambda E, h=h, d=d, bank=bank, ii=ii: E.matmul(
                    pool[bank][:, :], wfc_q[:, d * H + h * 128: d * H + h * 128 + 128], lnch(d),
                    start=(ii == 0), stop=(ii == ND - 1))
                if ii == ND - 1:
                    tu = b.emit("tensor", fn, inc="pe")
                else:
                    b.emit("tensor", fn)
            b.wait("scalar", tu)
            tge = b.emit("scalar", lambda E, h=h, bank=bank: E.activation(
                gch(h), pool[bank][:, :], AF.Gelu), inc="s")
            setfree(("pool", bank), tge)
            tok_gelu[(k, h)] = tge
            geng, ginc = ("gpsimd", "g") if k == 0 else ("vector", "v")
            b.wait(geng, tge)
            usewait(geng, ("gsq", h % 3))
            tgs = b.emit(geng, lambda E, h=h: E.tensor_tensor(
                gsq[:, (h % 3) * T:(h % 3 + 1) * T], gch(h), gch(h), ALU.mult), inc=ginc)
            gs_toks[h] = (tge, tgs)
            if k == 0:
                if h + 3 < NH:
                    emit_prep2b_load(h + 3)
                emit_prep2b_piece(h)
                if h >= 2:
                    emit_prep2b_fold(h - 2)
            if h >= 2:
                tq3 = emit_ln3_stat(h - 2)
        tq3 = emit_ln3_stat(NH - 2)
        tq3 = emit_ln3_stat(NH - 1)
        prev_rs3 = tok_yev.get((k - 1, ND - 1)) if k >= 1 else None
        if k == 0:
            emit_prep2b_tail()
        if k + 1 < nsup:
            emit_wri_load(k + 1, 0)
            emit_wri_load(k + 1, 1)
        emit_wp_load(k, 0)
        emit_wp_load(k, 1)
        emit_y_group(k, 0)
        emit_y_group(k, 1)
        tok_bc3 = rows_and_bcast(tq3, 3, rs3_t, rm3_t, prev_read=prev_rs3)
        tok_bc3_of[k] = tok_bc3
        setfree(("lnbf",), tu)
        if k + 1 < nsup:
            emit_ln1(k + 1)

    j = nsup - 1
    emit_y_evac(j, 0)
    for d in range(2, ND):
        emit_y_group(j, d)
        emit_y_evac(j, d - 1)
    emit_y_evac(j, ND - 1)
    if debug:
        DBG = {}
        for nm, t_ in (("d_carry", carry), ("d_lnbf", ln_bf), ("d_g", g_sb), ("d_thr", thr),
                       ("d_cpp", c_pp), ("d_wbar", wbar_pp), ("d_clam", clam05),
                       ("d_rs3", rs3_t), ("d_rm3", rm3_t), ("d_wfcq", wfc_q),
                       ("d_ta", t_a), ("d_tb", t_b)):
            sh = [t_.shape[0], t_.shape[1]]
            dt_ = t_.dtype
            Dd = nc.dram_tensor(nm, sh, dt_, kind="ExternalOutput")
            b.wait("sync", tok_yev[(j, ND - 1)])
            DBG[nm] = b.emit("sync", lambda E, Dd=Dd, t_=t_: E.dma_start(Dd[:, :], t_[:, :]), inc="d_y", n=16)
        b.wait("sync", list(DBG.values())[-1])
    b.wait("sync", tok_yev[(j, ND - 1)])
    sj, tj = j // 4, (j % 4) * T
    src = xping[j % 2][:, :].rearrange("p (c t) -> p c t", c=ND)
    dst = Yd[sj, :, tj:tj + T].rearrange("(c p) t -> p c t", p=128)
    tok_ystore[j] = b.emit("sync", lambda E, d=dst, s=src: E.dma_start(d, s), inc="d_y", n=16)
    b.wait("sync", ("d_y", b.semc["d_y"]))

    with nc.Block() as blk:
        @blk.sync
        def _(E):
            for f in b.prog["sync"]:
                f(E)

        @blk.tensor
        def _(E):
            for f in b.prog["tensor"]:
                f(E)

        @blk.vector
        def _(E):
            for f in b.prog["vector"]:
                f(E)

        @blk.scalar
        def _(E):
            for f in b.prog["scalar"]:
                f(E)

        @blk.gpsimd
        def _(E):
            for f in b.prog["gpsimd"]:
                f(E)

    es.close()
    return nc


_NC_CACHE = None


def kernel(L0_attn_x, L0_attn_W_r, L0_attn_W_i, L0_attn_lambd,
           L0_ffn_W_fc, L0_ffn_ln_w, L0_ffn_ln_b, L0_ffn_W_proj):
    global _NC_CACHE
    x = np.ascontiguousarray(np.asarray(L0_attn_x), dtype=np.float32)
    xt = np.ascontiguousarray(x.transpose(0, 2, 1))
    WrT = np.ascontiguousarray(np.asarray(L0_attn_W_r, dtype=np.float32).T)
    WiT = np.ascontiguousarray(np.asarray(L0_attn_W_i, dtype=np.float32).T)
    WfcT = np.ascontiguousarray(np.asarray(L0_ffn_W_fc, dtype=np.float32).T)
    WpT = np.ascontiguousarray(np.asarray(L0_ffn_W_proj, dtype=np.float32).T)
    lam = np.ascontiguousarray(np.asarray(L0_attn_lambd, dtype=np.float32).reshape(ND, 128).T)
    lnw = np.ascontiguousarray(np.asarray(L0_ffn_ln_w, dtype=np.float32).reshape(NH, 128).T)
    lnb = np.ascontiguousarray(np.asarray(L0_ffn_ln_b, dtype=np.float32).reshape(NH, 128).T)

    if _NC_CACHE is None:
        _NC_CACHE = build_nc()
    nc = _NC_CACHE

    in_maps = []
    for c in range(8):
        in_maps.append({
            "x": np.ascontiguousarray(xt[2 * c:2 * c + 2]),
            "wrt": WrT, "wit": WiT, "wfct": WfcT, "wpt": WpT,
            "lam": lam, "lnw": lnw, "lnb": lnb,
        })
    res = run_bass_kernel_spmd(nc, in_maps, core_ids=list(range(8)))
    y = np.concatenate([np.asarray(res.results[c]["y"]) for c in range(8)], axis=0)
    out = np.ascontiguousarray(y.transpose(0, 2, 1), dtype=np.float32)
    return out
